# version 01
# baseline (speedup 1.0000x reference)
import numpy as np
import ml_dtypes
import concourse.bass as bass
import concourse.mybir as mybir
from concourse.bass_utils import run_bass_kernel_spmd
from contextlib import ExitStack

F32 = mybir.dt.float32
BF16 = mybir.dt.bfloat16
I32 = mybir.dt.int32
AF = mybir.ActivationFunctionType
ALU = mybir.AluOpType
AX = mybir.AxisListType
IOA = bass.IndirectOffsetOnAxis

DIM = 2048
HID = 4096
STATE = 128
HEADS = 64
HEAD_DIM = 64
CONV_DIM = 4352
D_IN = 8512
T = 2048
TH = T + 3
CH = 128
NCHUNK = 16
CAP = 1024
NEXP = 8
MM = 1024

DEBUG = False

ENGINES = ("sync", "scalar", "vector", "tensor", "gpsimd")


class Reg:
    __slots__ = ("buf", "p0", "p1", "f0", "f1")

    def __init__(self, buf, p0, p1, f0, f1):
        self.buf, self.p0, self.p1, self.f0, self.f1 = buf, p0, p1, f0, f1

    def overlaps(self, o):
        return (self.buf is o.buf and self.p0 < o.p1 and o.p0 < self.p1
                and self.f0 < o.f1 and o.f0 < self.f1)


class KB:
    def __init__(self, nc):
        self.nc = nc
        self.ops = []
        self.sem_names = set()
        self.counts = {}
        self.hist = {}

    def op(self, engine, emit, reads=(), writes=(), dma=False, inc=None, tag=""):
        if inc is None:
            sem = ("d_" + engine) if dma else ("q_" + engine)
            amt = 16 if dma else 1
        else:
            sem, amt = inc
        self.sem_names.add(sem)
        cnt = self.counts.get(sem, 0) + amt
        self.counts[sem] = cnt
        waits = {}

        def need(ds, dc, de):
            if engine == "tensor" and de == "tensor":
                return
            if waits.get(ds, 0) < dc:
                waits[ds] = dc

        for r in reads:
            for (hr, kind, hs, hc, he) in self.hist.get(id(r.buf), ()):
                if kind == "w" and hr.overlaps(r):
                    need(hs, hc, he)
        for w in writes:
            for (hr, kind, hs, hc, he) in self.hist.get(id(w.buf), ()):
                if hr.overlaps(w):
                    need(hs, hc, he)
        rec = dict(engine=engine, emit=emit, waits=waits, sem=sem, amt=amt,
                   cnt=cnt, tag=tag)
        self.ops.append(rec)
        for r in reads:
            self.hist.setdefault(id(r.buf), []).append((r, "r", sem, cnt, engine))
        for w in writes:
            h = self.hist.setdefault(id(w.buf), [])
            h[:] = [e for e in h if not (w.p0 <= e[0].p0 and w.p1 >= e[0].p1
                    and w.f0 <= e[0].f0 and w.f1 >= e[0].f1)]
            h.append((w, "w", sem, cnt, engine))
        return rec

    def barrier(self, engine, waits):
        self.ops.append(dict(engine=engine, emit=None, waits=dict(waits),
                             sem=None, amt=0, cnt=0, tag="barrier"))
        for s in waits:
            self.sem_names.add(s)

    def finalize(self, es):
        sems = {n: es.enter_context(self.nc.semaphore(n))
                for n in sorted(self.sem_names)}
        streams = {e: [] for e in ENGINES}
        for rec in self.ops:
            streams[rec["engine"]].append(rec)
        with self.nc.Block() as block:
            for ename in ENGINES:
                recs = streams[ename]
                if not recs:
                    continue

                def body(eng, recs=recs):
                    water = {}
                    for rec in recs:
                        for sn, sc in sorted(rec["waits"].items()):
                            if water.get(sn, 0) < sc:
                                eng.wait_ge(sems[sn], sc)
                                water[sn] = sc
                        if rec["emit"] is not None:
                            rec["emit"](eng).then_inc(sems[rec["sem"]], rec["amt"])
                getattr(block, ename)(body)


class Arena:

    def __init__(self, nbytes):
        self.n = nbytes
        self.at = 0
        self.peak = 0
        self.marks = []

    def take(self, nbytes, align=64):
        a = (self.at + align - 1) // align * align
        assert a + nbytes <= self.n, f"SBUF arena overflow: {a}+{nbytes} > {self.n}"
        self.at = a + nbytes
        self.peak = max(self.peak, self.at)
        return a

    def mark(self):
        self.marks.append(self.at)

    def release(self):
        self.at = self.marks.pop()


def build_core_program():
    nc = bass.Bass("TRN2", target_bir_lowering=False)

    def DI(name, shape, dtype=F32):
        return nc.dram_tensor(name, shape, dtype, kind="ExternalInput")

    def DO(name, shape, dtype=F32):
        return nc.dram_tensor(name, shape, dtype, kind="ExternalOutput")

    def DS(name, shape, dtype):
        return nc.dram_tensor(name, shape, dtype)

    x_in = DI("x2051", [TH, DIM])
    winT = DI("winT", [DIM, D_IN])
    woutT = DI("woutT", [HID, DIM])
    convw = DI("convw", [CONV_DIM, 4])
    dt_bias = DI("dt_bias", [HEADS])
    a_log = DI("a_log", [HEADS])
    d_in = DI("d_vec", [HEADS])
    norm_w = DI("norm_w", [128, HID // 128])
    w_gate = DI("w_gate", [DIM, NEXP])
    wi0 = DI("wi0", [NEXP, DIM, MM])
    wi1 = DI("wi1", [NEXP, DIM, MM])
    wo = DI("wo", [NEXP, MM, DIM])
    ident_in = DI("ident", [128, 128])
    tri_in = DI("tri", [128, 128])
    rsel_in = DI("rsel", [64, HID], BF16)
    gates_in = DI("gates", [2])

    out_t = DO("out", [T, DIM])

    xn_dram = DS("xn_dram", [2064, DIM], BF16)
    conv_dump = DS("conv_dump", [CONV_DIM, T], BF16)
    z_dram = DS("z_dram", [T, HID], BF16)
    ylocal_dram = DS("ylocal_dram", [T, HID], F32)
    ynorm_dram = DS("ynorm_dram", [T, HID], BF16)
    s_dram = DS("s_dram", [T, DIM], F32)
    xsorted = DS("xsorted", [NEXP * CAP, DIM], BF16)
    outsorted = DS("outsorted", [NEXP * CAP, DIM], F32)
    idx_dram = DS("idx_dram", [NEXP * CAP, 1], I32)
    rowdump = DS("rowdump", [NCHUNK, HEADS * CH], F32)
    smalldump = DS("smalldump", [4096], F32)
    cc_in = DS("cc_in", [STATE, HID], F32)
    cc_out = DS("cc_out", [STATE, HID], F32)

    dbg = {}
    if DEBUG:
        dbg["xn"] = DO("dbg_xn", [T, DIM], BF16)
        dbg["xbc"] = DO("dbg_xbc", [CONV_DIM, T], BF16)
        dbg["dt"] = DO("dbg_dt", [HEADS, T])
        dbg["dac"] = DO("dbg_dac", [HEADS, T])
        dbg["z"] = DO("dbg_z", [T, HID], BF16)
        dbg["logits"] = DO("dbg_logits", [T, NEXP])
        dbg["ylocal"] = DO("dbg_ylocal", [T, HID])
        dbg["ynorm"] = DO("dbg_ynorm", [T, HID], BF16)
        dbg["gsum"] = DO("dbg_gsum", [T, DIM])
        dbg["sproj"] = DO("dbg_sproj", [T, DIM])
        dbg["carry"] = DO("dbg_carry", [STATE, HID])
        dbg["w12"] = DO("dbg_w12", [128, 32])
        dbg["slots"] = DO("dbg_slots", [128, 32])
        dbg["dlt"] = DO("dbg_dlt", [128, 1024])
        dbg["lndtt"] = DO("dbg_lndtt", [128, 1024])
        dbg["dtdst"] = DO("dbg_dtdst", [128, 1024])
        dbg["eyfmb"] = DO("dbg_eyfmb", [64, 2048], BF16)
        dbg["cdbc"] = DO("dbg_cdbc", [128, 1024])
        dbg["idx"] = DO("dbg_idx", [NEXP * CAP, 1], I32)
        dbg["xs"] = DO("dbg_xs", [NEXP * CAP, DIM], BF16)
        dbg["os"] = DO("dbg_os", [NEXP * CAP, DIM])

    es = ExitStack()
    kb = KB(nc)

    ARB = 168 * 1024
    ar_h = es.enter_context(nc.sbuf_tensor("AR", [128, ARB // 4], F32))
    psA = es.enter_context(nc.psum_tensor("psA", [128, 2048], F32))
    psB = es.enter_context(nc.psum_tensor("psB", [128, 2048], F32))
    ix_h = es.enter_context(nc.sbuf_tensor("IX", [128, 128], I32))
    ws_h = es.enter_context(nc.sbuf_tensor("WS", [128, 128], F32))
    A = Arena(ARB)

    def RX(c0, c1, p0=0, p1=128):
        return Reg(ix_h, p0, p1, c0, c1)

    def RW(c0, c1, p0=0, p1=128):
        return Reg(ws_h, p0, p1, c0, c1)

    def wap(c0, n, pn=128):
        return ws_h[0:pn, c0:c0 + n]

    def fap(off, pn, n, p0=0):
        assert off % 4 == 0
        return ar_h[p0:p0 + pn, off // 4: off // 4 + n]

    def bap(off, pn, n, p0=0):
        assert off % 4 == 0 and n % 2 == 0
        return ar_h[p0:p0 + pn, off // 4: off // 4 + n // 2].bitcast(BF16)

    def iap(off, pn, n, p0=0):
        assert off % 4 == 0
        return ar_h[p0:p0 + pn, off // 4: off // 4 + n].bitcast(I32)

    def RA(off, nbytes, p0=0, p1=128):
        return Reg(ar_h, p0, p1, off, off + nbytes)

    def RP(ps, c0, c1, p0=0, p1=128):
        return Reg(ps, p0, p1, c0, c1)

    def RD(t, r0, r1, c0=0, c1=1 << 30):
        return Reg(t, r0, r1, c0, c1)

    def dma(q, out_ap, in_ap, reads, writes, tag="", **kw):
        return kb.op(q, lambda e: e.dma_start(out=out_ap, in_=in_ap, **kw),
                     reads=reads, writes=writes, dma=True, tag=tag)

    def dmaT(q, out_ap, in_ap, reads, writes, tag=""):
        return kb.op(q, lambda e: e.dma_start(out=out_ap, in_=in_ap,
                                              transpose=True),
                     reads=reads, writes=writes, dma=True, tag=tag)

    def vop(emit, reads, writes, tag=""):
        return kb.op("vector", emit, reads=reads, writes=writes, tag=tag)

    def sop(emit, reads, writes, tag=""):
        return kb.op("scalar", emit, reads=reads, writes=writes, tag=tag)

    def gop(emit, reads, writes, tag=""):
        return kb.op("gpsimd", emit, reads=reads, writes=writes, tag=tag)

    def mmg(items, reads, writes, tag=""):
        def emit(eng):
            ins = None
            for (o, l, r, st, sp) in items:
                ins = eng.matmul(o, l, r, start=st, stop=sp)
            return ins
        return kb.op("tensor", emit, reads=reads, writes=writes, tag=tag)

    def trans(out_ap, in_ap, ident_ap, reads, writes, tag=""):
        return kb.op("tensor",
                     lambda e: e.transpose(out_ap, in_ap, ident_ap),
                     reads=reads, writes=writes, tag=tag)

    NR = ()

    bnd_reg = nc.gpsimd.alloc_register("bndreg")
    kb.op("gpsimd", lambda e: e.reg_mov(bnd_reg, NEXP * CAP - 1), NR, NR,
          tag="bndmov")
    c_ident = A.take(128 * 4)
    c_identb = A.take(128 * 2)
    c_trib = A.take(128 * 4)
    c_rsel = A.take(HID * 2)
    c_convw = A.take(136 * 4)
    c_dbias = A.take(4)
    c_nA = A.take(4)
    c_dcol = A.take(64 * 4)
    c_nw = A.take(32 * 4)
    c_gates = A.take(2 * 4)
    c_wgate = A.take(128 * 2)
    c_zero = A.take(128 * 4)
    c_eps = A.take(4)
    c_ones1 = A.take(128 * 2)

    r_ident = RA(c_ident, 512)
    dma("sync", fap(c_ident, 128, 128), ident_in.ap(), NR, [r_ident])
    r_identb = RA(c_identb, 256)
    dma("gpsimd", bap(c_identb, 128, 128), ident_in.ap(), NR, [r_identb])
    r_trib = RA(c_trib, 512)
    dma("sync", fap(c_trib, 128, 128), tri_in.ap(), NR, [r_trib])
    r_rsel = RA(c_rsel, HID * 2, 0, 64)
    dma("sync", bap(c_rsel, 64, HID), rsel_in.ap(), NR, [r_rsel])
    r_convw = RA(c_convw, 544)
    dma("sync", fap(c_convw, 128, 136),
        bass.AP(convw, 0, [[4, 128], [512, 34], [1, 4]]), NR, [r_convw])
    r_dbias = RA(c_dbias, 4, 0, 64)
    dma("sync", fap(c_dbias, 64, 1), bass.AP(dt_bias, 0, [[1, 64], [1, 1]]),
        NR, [r_dbias])
    r_nA = RA(c_nA, 4, 0, 64)
    dma("sync", fap(c_nA, 64, 1), bass.AP(a_log, 0, [[1, 64], [1, 1]]),
        NR, [r_nA])
    sop(lambda e: e.activation(out=fap(c_nA, 64, 1), in_=fap(c_nA, 64, 1),
                               func=AF.Exp), [r_nA], [r_nA], tag="expA")
    vop(lambda e: e.tensor_scalar(out=fap(c_nA, 64, 1), in0=fap(c_nA, 64, 1),
                                  scalar1=-1.0, scalar2=None, op0=ALU.mult),
        [r_nA], [r_nA], tag="negA")
    r_dcol = RA(c_dcol, 256)
    dma("sync", fap(c_dcol, 128, 64), bass.AP(d_in, 0, [[0, 128], [1, 64]]),
        NR, [r_dcol])
    r_nw = RA(c_nw, 128)
    dma("sync", fap(c_nw, 128, 32), norm_w.ap(), NR, [r_nw])
    r_gates = RA(c_gates, 8)
    dma("sync", fap(c_gates, 128, 2), bass.AP(gates_in, 0, [[0, 128], [1, 2]]),
        NR, [r_gates])
    r_wgate = RA(c_wgate, 256)
    dma("gpsimd", bap(c_wgate, 128, 128),
        bass.AP(w_gate, 0, [[8, 128], [1024, 16], [1, 8]]), NR, [r_wgate])
    r_zero = RA(c_zero, 512)
    vop(lambda e: e.memset(fap(c_zero, 128, 128), 0.0), NR, [r_zero])
    r_eps = RA(c_eps, 4)
    vop(lambda e: e.memset(fap(c_eps, 128, 1), 1e-5), NR, [r_eps])

    A.mark()
    p0_x = [A.take(2048 * 4) for _ in range(2)]
    p0_sq = [A.take(2048 * 4) for _ in range(2)]
    p0_st = [A.take(8 * 4) for _ in range(2)]
    p0_xn = [A.take(2048 * 2) for _ in range(2)]

    def ln_tile(i, pn, row):
        bx, bq, bs, bn = p0_x[i % 2], p0_sq[i % 2], p0_st[i % 2], p0_xn[i % 2]
        rx, rq = RA(bx, 8192, 0, pn), RA(bq, 8192, 0, pn)
        rs, rxn = RA(bs, 32, 0, pn), RA(bn, 4096, 0, pn)
        dma("sync", fap(bx, pn, 2048), x_in.ap()[row:row + pn, :], NR, [rx])
        vop(lambda e: e.reduce_sum(fap(bs, pn, 1), fap(bx, pn, 2048), axis=AX.X),
            [rx], [rs])
        sop(lambda e: e.activation(out=fap(bq, pn, 2048), in_=fap(bx, pn, 2048),
                                   func=AF.Square, accum_out=fap(bs + 4, pn, 1)),
            [rx], [rq, rs])
        vop(lambda e: e.tensor_scalar(out=fap(bs, pn, 2), in0=fap(bs, pn, 2),
                                      scalar1=1.0 / 2048, scalar2=None,
                                      op0=ALU.mult), [rs], [rs])
        vop(lambda e: e.tensor_scalar(out=fap(bs + 8, pn, 1), in0=fap(bs, pn, 1),
                                      scalar1=fap(bs, pn, 1), scalar2=None,
                                      op0=ALU.mult), [rs], [rs])
        vop(lambda e: e.tensor_sub(fap(bs + 12, pn, 1), fap(bs + 4, pn, 1),
                                   fap(bs + 8, pn, 1)), [rs], [rs])
        sop(lambda e: e.activation(out=fap(bs + 16, pn, 1), in_=fap(bs + 12, pn, 1),
                                   func=AF.Ln, bias=fap(c_eps, pn, 1)),
            [rs, r_eps], [rs])
        sop(lambda e: e.activation(out=fap(bs + 16, pn, 1), in_=fap(bs + 16, pn, 1),
                                   func=AF.Exp, scale=-0.5), [rs], [rs])
        vop(lambda e: e.tensor_scalar(out=bap(bn, pn, 2048), in0=fap(bx, pn, 2048),
                                      scalar1=fap(bs, pn, 1),
                                      scalar2=fap(bs + 16, pn, 1),
                                      op0=ALU.subtract, op1=ALU.mult),
            [rx, rs], [rxn])
        dma("sync", xn_dram.ap()[row:row + pn, :], bap(bn, pn, 2048),
            [rxn], [RD(xn_dram, row, row + pn)])

    ln_tile(0, 3, 0)
    for t in range(16):
        ln_tile(t + 1, 128, 3 + t * 128)
    A.release()


    dl_t = A.take(1024 * 4)
    lndt_t = A.take(1024 * 4)
    dtds_t = A.take(1024 * 4)
    ey_fmb = A.take(2048 * 2)
    cd_fm = A.take(16 * 4)
    cdp_fm = A.take(16 * 4)
    cd_bc = A.take(1024 * 4)
    carry = A.take(HID * 4)
    slot12 = A.take(32 * 4)
    lg_sb = A.take(128 * 4)
    bfm = A.take(2048 * 2)
    cfm = A.take(2048 * 2)
    r_dlt, r_lndtt, r_dtdst = RA(dl_t, 4096), RA(lndt_t, 4096), RA(dtds_t, 4096)
    r_eyfmb = RA(ey_fmb, 4096, 0, 64)
    r_cdfm, r_cdpfm = RA(cd_fm, 64, 0, 64), RA(cdp_fm, 64, 0, 64)
    r_cdbc = RA(cd_bc, 4096)
    r_carry = RA(carry, 16384)
    r_w12, r_slot12 = RW(0, 32), RA(slot12, 128)
    r_lg = RA(lg_sb, 512)
    r_bfm, r_cfm = RA(bfm, 4096), RA(cfm, 4096)
    dtdump = DS("dtdump", [64, T], F32)

    A.mark()

    xn_fm = A.take(16 * 2064 * 2)

    def xnf(kc, f0, n):
        return bap(xn_fm + kc * 2064 * 2, 128, 2064)[:, f0:f0 + n]

    r_xnfm = RA(xn_fm, 16 * 2064 * 2)
    for kc in range(16):
        dmaT("scalar", bap(xn_fm + kc * 2064 * 2, 128, 2064),
             xn_dram.ap()[0:2064, kc * 128:(kc + 1) * 128],
             [RD(xn_dram, 0, 2064)], [RA(xn_fm + kc * 2064 * 2, 2064 * 2)],
             tag=f"xnfm{kc}")

    A.mark()
    p1_ws = [A.take(2048 * 2) for _ in range(2)]
    p1_cb = [A.take(2052 * 2) for _ in range(2)]
    p1_t = A.take(2048 * 4)
    p1_co = A.take(2048 * 2)

    TOKG = [(0, 512), (512, 512), (1024, 512), (1536, 512), (2048, 3)]

    def conv_tile(ft, xbc_off, rxbc, out_off, rout):
        tb = p1_t
        rt = RA(tb, 8192)
        for i in range(4):
            wcol = fap(c_convw + (ft * 4 + i) * 4, 128, 1)
            src = bap(xbc_off, 128, 2052)[:, i:i + 2048]
            if i == 0:
                vop(lambda e, s=src, w=wcol: e.tensor_scalar(
                    out=fap(tb, 128, 2048), in0=s, scalar1=w, scalar2=None,
                    op0=ALU.mult), [rxbc, r_convw], [rt], tag=f"cv{ft}_{i}")
            else:
                vop(lambda e, s=src, w=wcol: e.scalar_tensor_tensor(
                    out=fap(tb, 128, 2048), in0=s, scalar=w,
                    in1=fap(tb, 128, 2048), op0=ALU.mult, op1=ALU.add),
                    [rxbc, rt, r_convw], [rt], tag=f"cv{ft}_{i}")
        sop(lambda e: e.activation(out=bap(out_off, 128, 2048),
                                   in_=fap(tb, 128, 2048), func=AF.Silu),
            [rt], [rout], tag=f"cvs{ft}")

    for ft in range(34):
        ws = p1_ws[ft % 2]
        rws = RA(ws, 4096)
        dma("gpsimd", bap(ws, 128, 2048),
            bass.AP(winT, 4096 + ft * 128,
                    [[D_IN, 128], [128 * D_IN, 16], [1, 128]]),
            NR, [rws], tag=f"wsl{ft}")
        for gi, (t0, tn) in enumerate(TOKG):
            ps = psA if gi < 4 else psB
            c0 = (t0 if gi < 4 else 0)
            items = []
            for kc in range(16):
                items.append((ps[0:128, c0:c0 + tn],
                              bap(ws, 128, 2048)[:, kc * 128:(kc + 1) * 128],
                              xnf(kc, t0, tn), kc == 0, kc == 15))
            mmg(items, [rws, r_xnfm], [RP(ps, c0, c0 + tn)], tag=f"ip{ft}_{gi}")
        cb = p1_cb[ft % 2]
        rcb = RA(cb, 2052 * 2)
        vop(lambda e, cb=cb: e.tensor_copy(bap(cb, 128, 2052)[:, 0:2048],
                                           psA[0:128, 0:2048]),
            [RP(psA, 0, 2048)], [rcb], tag=f"ev{ft}")
        vop(lambda e, cb=cb: e.tensor_copy(bap(cb, 128, 2052)[:, 2048:2051],
                                           psB[0:128, 0:3]),
            [RP(psB, 0, 3)], [rcb], tag=f"evh{ft}")
        if ft == 32:
            conv_tile(ft, cb, rcb, bfm, r_bfm)
            dma("sync", conv_dump.ap()[ft * 128:(ft + 1) * 128, :],
                bap(bfm, 128, 2048), [r_bfm],
                [RD(conv_dump, ft * 128, (ft + 1) * 128)], tag=f"cd{ft}")
        elif ft == 33:
            conv_tile(ft, cb, rcb, cfm, r_cfm)
        else:
            conv_tile(ft, cb, rcb, p1_co, RA(p1_co, 4096))
            dma("sync", conv_dump.ap()[ft * 128:(ft + 1) * 128, :],
                bap(p1_co, 128, 2048), [RA(p1_co, 4096)],
                [RD(conv_dump, ft * 128, (ft + 1) * 128)], tag=f"cd{ft}")

    ws = p1_ws[0]
    rws = RA(ws, 4096)
    dma("gpsimd", bap(ws, 128, 1024),
        bass.AP(winT, 8448, [[D_IN, 128], [128 * D_IN, 16], [1, 64]]),
        NR, [rws], tag="wsl_dt")
    for g in range(4):
        items = []
        for kc in range(16):
            items.append((psA[0:64, g * 512:(g + 1) * 512],
                          bap(ws, 128, 1024)[:, kc * 64:(kc + 1) * 64],
                          xnf(kc, 3 + g * 512, 512), kc == 0, kc == 15))
        mmg(items, [rws, r_xnfm], [RP(psA, g * 512, (g + 1) * 512, 0, 64)],
            tag=f"dtg{g}")
    vop(lambda e: e.tensor_copy(fap(p1_t, 64, 2048), psA[0:64, 0:2048]),
        [RP(psA, 0, 2048, 0, 64)], [RA(p1_t, 8192, 0, 64)], tag="dtev")
    dma("sync", dtdump.ap(), fap(p1_t, 64, 2048), [RA(p1_t, 8192, 0, 64)],
        [RD(dtdump, 0, 64)], tag="dtst")
    A.release()

    A.mark()
    p2_zs = [A.take(8192 * 2) for _ in range(2)]
    p2_zb = [A.take(512 * 2) for _ in range(2)]

    for tt in range(16):
        items = []
        for kc in range(16):
            items.append((psB[0:128, tt * 8:tt * 8 + 8],
                          xnf(kc, 3 + tt * 128, 128),
                          bap(c_wgate, 128, 128)[:, kc * 8:(kc + 1) * 8],
                          kc == 0, kc == 15))
        mmg(items, [r_xnfm, r_wgate], [RP(psB, tt * 8, tt * 8 + 8)],
            tag=f"lg{tt}")
    vop(lambda e: e.tensor_copy(fap(lg_sb, 128, 128), psB[0:128, 0:128]),
        [RP(psB, 0, 128)], [r_lg], tag="lgev")

    for fg in range(8):
        zs = p2_zs[fg % 2]
        rzs = RA(zs, 16384)
        dma("gpsimd", bap(zs, 128, 8192),
            bass.AP(winT, fg * 512, [[D_IN, 128], [128 * D_IN, 16], [1, 512]]),
            NR, [rzs], tag=f"zw{fg}")
        for tt in range(16):
            c0 = (tt % 4) * 512
            items = []
            for kc in range(16):
                items.append((psA[0:128, c0:c0 + 512],
                              xnf(kc, 3 + tt * 128, 128),
                              bap(zs, 128, 8192)[:, kc * 512:(kc + 1) * 512],
                              kc == 0, kc == 15))
            mmg(items, [rzs, r_xnfm], [RP(psA, c0, c0 + 512)], tag=f"z{fg}_{tt}")
            zb = p2_zb[tt % 2]
            rzb = RA(zb, 1024)
            vop(lambda e, zb=zb, c0=c0: e.tensor_copy(bap(zb, 128, 512),
                                                      psA[0:128, c0:c0 + 512]),
                [RP(psA, c0, c0 + 512)], [rzb], tag=f"zev{fg}_{tt}")
            dma("sync", z_dram.ap()[tt * 128:(tt + 1) * 128, fg * 512:(fg + 1) * 512],
                bap(zb, 128, 512), [rzb],
                [RD(z_dram, tt * 128, (tt + 1) * 128, fg * 512, (fg + 1) * 512)],
                tag=f"zst{fg}_{tt}")
    A.release()
    A.release()
    if DEBUG:
        dma("gpsimd", dbg["logits"].ap(), fap(lg_sb, 128, 128), [r_lg],
            [RD(dbg["logits"], 0, 128)])

    A.mark()
    b_dt = A.take(2048 * 4)
    b_t1 = A.take(2048 * 4)
    b_t2 = A.take(2048 * 4)
    b_dac = A.take(2048 * 4)
    b_lndt = A.take(2048 * 4)
    b_dl = A.take(2048 * 4)
    b_sml = A.take(256 * 4)
    r_bdt = RA(b_dt, 8192, 0, 64)
    r_bt1 = RA(b_t1, 8192, 0, 64)
    r_bt2 = RA(b_t2, 8192, 0, 64)
    r_bdac = RA(b_dac, 8192, 0, 64)
    r_blndt = RA(b_lndt, 8192, 0, 64)
    r_bdl = RA(b_dl, 8192, 0, 64)
    r_bsml = RA(b_sml, 1024, 0, 64)

    dma("sync", fap(b_t1, 64, 2048), dtdump.ap(), [RD(dtdump, 0, 64)], [r_bt1],
        tag="dtld")
    vop(lambda e: e.tensor_scalar(out=fap(b_dt, 64, 2048), in0=fap(b_t1, 64, 2048),
                                  scalar1=fap(c_dbias, 64, 1), scalar2=None,
                                  op0=ALU.add), [r_bt1, r_dbias], [r_bdt], tag="xb")
    sop(lambda e: e.activation(out=fap(b_t2, 64, 2048), in_=fap(b_dt, 64, 2048),
                               func=AF.Abs), [r_bdt], [r_bt2], tag="sp_abs")
    sop(lambda e: e.activation(out=fap(b_t1, 64, 2048), in_=fap(b_t2, 64, 2048),
                               func=AF.Exp, scale=-1.0), [r_bt2], [r_bt1], tag="sp_exp")
    sop(lambda e: e.activation(out=fap(b_t2, 64, 2048), in_=fap(b_t1, 64, 2048),
                               func=AF.Ln, bias=1.0), [r_bt1], [r_bt2], tag="sp_ln")
    vop(lambda e: e.tensor_scalar(out=fap(b_t1, 64, 2048), in0=fap(b_dt, 64, 2048),
                                  scalar1=0.0, scalar2=None, op0=ALU.max),
        [r_bdt], [r_bt1], tag="sp_relu")
    vop(lambda e: e.tensor_add(fap(b_dt, 64, 2048), fap(b_t1, 64, 2048),
                               fap(b_t2, 64, 2048)), [r_bt1, r_bt2], [r_bdt],
        tag="sp_add")
    if DEBUG:
        dma("gpsimd", dbg["dt"].ap(), fap(b_dt, 64, 2048), [r_bdt],
            [RD(dbg["dt"], 0, 64)])
    sop(lambda e: e.activation(out=fap(b_lndt, 64, 2048), in_=fap(b_dt, 64, 2048),
                               func=AF.Ln), [r_bdt], [r_blndt], tag="lndt")
    vop(lambda e: e.tensor_scalar(out=fap(b_t1, 64, 2048), in0=fap(b_dt, 64, 2048),
                                  scalar1=fap(c_nA, 64, 1), scalar2=None,
                                  op0=ALU.mult), [r_bdt, r_nA], [r_bt1], tag="dA")
    for c in range(NCHUNK):
        vop(lambda e, c=c: e.tensor_tensor_scan(
            out=fap(b_dac + c * 512, 64, 128), data0=fap(b_t1 + c * 512, 64, 128),
            data1=fap(c_zero, 64, 128), initial=0.0, op0=ALU.add, op1=ALU.add),
            [r_bt1, r_zero], [r_bdac], tag=f"dac{c}")
    if DEBUG:
        dma("gpsimd", dbg["dac"].ap(), fap(b_dac, 64, 2048), [r_bdac],
            [RD(dbg["dac"], 0, 64)])
    vop(lambda e: e.tensor_sub(fap(b_dl, 64, 2048), fap(b_dac, 64, 2048),
                               fap(b_lndt, 64, 2048)), [r_bdac, r_blndt], [r_bdl],
        tag="dl")
    sop(lambda e: e.activation(out=bap(ey_fmb, 64, 2048), in_=fap(b_dac, 64, 2048),
                               func=AF.Exp), [r_bdac], [r_eyfmb], tag="ey")
    for c in range(NCHUNK):
        sop(lambda e, c=c: e.activation(out=fap(cd_fm + c * 4, 64, 1),
                                        in_=fap(b_dac + c * 512 + 508, 64, 1),
                                        func=AF.Exp), [r_bdac], [r_cdfm],
            tag=f"cd{c}")
    vop(lambda e: e.memset(fap(cdp_fm, 64, 1), 1.0), NR, [r_cdpfm], tag="cdp0")
    vop(lambda e: e.tensor_tensor_scan(
        out=fap(cdp_fm + 4, 64, 15), data0=fap(cd_fm, 64, 15),
        data1=fap(c_zero, 64, 15), initial=1.0, op0=ALU.mult, op1=ALU.add),
        [r_cdfm, r_zero], [r_cdpfm], tag="cdp")
    for c in range(NCHUNK):
        dma("sync", bass.AP(rowdump, c * 8192, [[128, 64], [1, 128]]),
            fap(b_dac + c * 512, 64, 128), [r_bdac], [RD(rowdump, c, c + 1)],
            tag=f"rd{c}")
    dma("sync", bass.AP(smalldump, 0, [[16, 64], [1, 16]]), fap(cd_fm, 64, 16),
        [r_cdfm], [RD(smalldump, 0, 1024)], tag="cddump")
    dma("sync", fap(cd_bc, 128, 1024), bass.AP(smalldump, 0, [[0, 128], [1, 1024]]),
        [RD(smalldump, 0, 1024)], [r_cdbc], tag="cdbc")
    for c in range(NCHUNK):
        vop(lambda e, c=c: e.tensor_scalar(
            out=fap(b_sml, 64, 128), in0=fap(b_dac + c * 512, 64, 128),
            scalar1=fap(b_dac + c * 512 + 127 * 4, 64, 1), scalar2=None,
            op0=ALU.subtract), [r_bdac], [r_bsml], tag=f"ds{c}")
        sop(lambda e: e.activation(out=fap(b_sml, 64, 128), in_=fap(b_sml, 64, 128),
                                   func=AF.Exp, scale=-1.0), [r_bsml], [r_bsml],
            tag=f"dse{c}")
        vop(lambda e, c=c: e.tensor_mul(fap(b_sml + 512, 64, 128),
                                        fap(b_dt + c * 512, 64, 128),
                                        fap(b_sml, 64, 128)),
            [r_bdt, r_bsml], [r_bsml], tag=f"dtds{c}")
        for (src, dst, rdst) in ((b_dl + c * 512, dl_t + c * 256, r_dlt),
                                 (b_lndt + c * 512, lndt_t + c * 256, r_lndtt),
                                 (b_sml + 512, dtds_t + c * 256, r_dtdst)):
            kb.op("tensor",
                  lambda e, src=src: e.transpose(psB[0:128, 1920:1984],
                                                 fap(src, 64, 128),
                                                 fap(c_ident, 64, 64)),
                  reads=[RA(src, 512, 0, 64), r_ident],
                  writes=[RP(psB, 1920, 1984)], tag=f"tr{c}")
            vop(lambda e, dst=dst: e.tensor_copy(fap(dst, 128, 64),
                                                 psB[0:128, 1920:1984]),
                [RP(psB, 1920, 1984)], [rdst], tag=f"trc{c}")
    if DEBUG:
        dma("gpsimd", dbg["dlt"].ap(), fap(dl_t, 128, 1024), [r_dlt],
            [RD(dbg["dlt"], 0, 128)])
        dma("gpsimd", dbg["lndtt"].ap(), fap(lndt_t, 128, 1024), [r_lndtt],
            [RD(dbg["lndtt"], 0, 128)])
        dma("gpsimd", dbg["dtdst"].ap(), fap(dtds_t, 128, 1024), [r_dtdst],
            [RD(dbg["dtdst"], 0, 128)])
        dma("gpsimd", dbg["eyfmb"].ap(), bap(ey_fmb, 64, 2048), [r_eyfmb],
            [RD(dbg["eyfmb"], 0, 64)])
        dma("gpsimd", dbg["cdbc"].ap(), fap(cd_bc, 128, 1024), [r_cdbc],
            [RD(dbg["cdbc"], 0, 128)])
    A.release()

    A.mark()
    p3_mask1 = A.take(8 * 4)
    p3_mask2 = A.take(8 * 4)
    p3_lg2 = A.take(8 * 4)
    p3_m12 = A.take(8 * 4)
    p3_em = A.take(2048 * 4)
    p3_pos = A.take(2048 * 4)
    p3_g0 = A.take(2048 * 4)
    p3_ov = A.take(2048 * 4)
    p3_gsc = A.take(2048 * 4)
    p3_eb = A.take(16 * 4)
    p3_tm = A.take(64 * 4)
    p3_i = A.take(64 * 4)
    r_m = RW(32, 96)
    r_mask1, r_mask2 = RA(p3_mask1, 32), RA(p3_mask2, 32)
    r_lg2, r_m12 = RA(p3_lg2, 32), RA(p3_m12, 32)
    r_em = RA(p3_em, 8192, 0, 8)
    r_pos = RA(p3_pos, 8192, 0, 8)
    r_g0 = RA(p3_g0, 8192, 0, 8)
    r_ov = RA(p3_ov, 8192, 0, 8)
    r_gsc = RA(p3_gsc, 8192, 0, 8)
    r_eb = RA(p3_eb, 64, 0, 8)
    r_tm = RA(p3_tm, 256)
    r_i3 = RA(p3_i, 256)

    gop(lambda e: e.memset(ix_h[:, 0:64], 3), NR, [RX(0, 64)], tag="pf0")
    dma("gpsimd", bass.AP(idx_dram, 0, [[64, 128], [1, 64]]),
        ix_h[:, 0:64], [RX(0, 64)], [RD(idx_dram, 0, 8192)], tag="pf1")

    for tt in range(16):
        lg = fap(lg_sb, 128, 128)[:, tt * 8:(tt + 1) * 8]
        mb = 32 + tt * 4
        vop(lambda e, lg=lg, mb=mb: e.reduce_max(wap(mb, 1), lg, axis=AX.X),
            [r_lg], [r_m], tag=f"r_m1_{tt}")
        vop(lambda e, lg=lg, mb=mb: e.tensor_scalar(
            out=fap(p3_mask1, 128, 8), in0=lg, scalar1=wap(mb, 1),
            scalar2=None, op0=ALU.is_equal), [r_lg, r_m], [r_mask1], tag=f"r_k1_{tt}")
        vop(lambda e, lg=lg: e.scalar_tensor_tensor(
            out=fap(p3_lg2, 128, 8), in0=fap(p3_mask1, 128, 8), scalar=-1e30,
            in1=lg, op0=ALU.mult, op1=ALU.add), [r_lg, r_mask1], [r_lg2],
            tag=f"r_l2_{tt}")
        vop(lambda e, mb=mb: e.reduce_max(wap(mb + 1, 1), fap(p3_lg2, 128, 8),
                                          axis=AX.X), [r_lg2], [r_m], tag=f"r_m2_{tt}")
        vop(lambda e, mb=mb: e.tensor_scalar(
            out=fap(p3_mask2, 128, 8), in0=fap(p3_lg2, 128, 8),
            scalar1=wap(mb + 1, 1), scalar2=None, op0=ALU.is_equal),
            [r_lg2, r_m], [r_mask2], tag=f"r_k2_{tt}")
        vop(lambda e, mb=mb: e.tensor_sub(wap(mb + 2, 1), wap(mb + 1, 1),
                                          wap(mb, 1)), [r_m], [r_m],
            tag=f"r_dx_{tt}")
        sop(lambda e, mb=mb: e.activation(out=wap(mb + 2, 1),
                                          in_=wap(mb + 2, 1), func=AF.Exp),
            [r_m], [r_m], tag=f"r_ed_{tt}")
        vop(lambda e, mb=mb: e.tensor_scalar(out=wap(mb + 3, 1),
                                             in0=wap(mb + 2, 1), scalar1=1.0,
                                             scalar2=None, op0=ALU.add),
            [r_m], [r_m], tag=f"r_od_{tt}")
        vop(lambda e, mb=mb, tt=tt: e.reciprocal(wap(tt * 2, 1),
                                          wap(mb + 3, 1)), [r_m], [r_w12],
            tag=f"r_w1_{tt}")
        vop(lambda e, mb=mb, tt=tt: e.tensor_mul(wap(tt * 2 + 1, 1),
                                                 wap(mb + 2, 1),
                                                 wap(tt * 2, 1)),
            [r_m, r_w12], [r_w12], tag=f"r_w2_{tt}")
        vop(lambda e: e.tensor_add(fap(p3_m12, 128, 8), fap(p3_mask1, 128, 8),
                                   fap(p3_mask2, 128, 8)), [r_mask1, r_mask2],
            [r_m12], tag=f"r_s_{tt}")
        kb.op("tensor", lambda e: e.transpose(psB[0:8, 1664:1792],
                                              fap(p3_m12, 128, 8),
                                              fap(c_ident, 128, 128)),
              reads=[r_m12, r_ident], writes=[RP(psB, 1664, 1792, 0, 8)],
              tag=f"r_t_{tt}")
        vop(lambda e, tt=tt: e.tensor_copy(fap(p3_em + tt * 512, 8, 128),
                                           psB[0:8, 1664:1792]),
            [RP(psB, 1664, 1792, 0, 8)], [r_em], tag=f"r_e_{tt}")

    vop(lambda e: e.memset(fap(p3_g0, 8, 2048), 0.0), NR, [r_g0], tag="pz")
    vop(lambda e: e.tensor_tensor_scan(out=fap(p3_pos, 8, 2048),
                                       data0=fap(p3_em, 8, 2048),
                                       data1=fap(p3_g0, 8, 2048),
                                       initial=0.0, op0=ALU.add, op1=ALU.add),
        [r_em, r_g0], [r_pos], tag="pos")
    gop(lambda e: e.iota(iap(p3_eb + 32, 8, 1), [[1, 1]], channel_multiplier=1024),
        NR, [r_eb], tag="eb_i")
    vop(lambda e: e.tensor_copy(fap(p3_eb, 8, 1), iap(p3_eb + 32, 8, 1)),
        [r_eb], [r_eb], tag="eb_f")
    vop(lambda e: e.tensor_scalar(out=fap(p3_eb + 4, 8, 1), in0=fap(p3_eb, 8, 1),
                                  scalar1=1023.0, scalar2=None, op0=ALU.add),
        [r_eb], [r_eb], tag="eb1023")
    vop(lambda e: e.tensor_scalar(out=fap(p3_g0, 8, 2048), in0=fap(p3_pos, 8, 2048),
                                  scalar1=fap(p3_eb, 8, 1), scalar2=1.0,
                                  op0=ALU.add, op1=ALU.subtract),
        [r_pos, r_eb], [r_g0], tag="g0")
    vop(lambda e: e.tensor_scalar(out=fap(p3_ov, 8, 2048), in0=fap(p3_pos, 8, 2048),
                                  scalar1=1024.5, scalar2=None, op0=ALU.is_ge),
        [r_pos], [r_ov], tag="ovv")
    vop(lambda e: e.tensor_scalar(out=fap(p3_gsc, 8, 2048),
                                  in0=fap(p3_em, 8, 2048), scalar1=-1.0,
                                  scalar2=1.0, op0=ALU.mult, op1=ALU.add),
        [r_em], [r_gsc], tag="gsc0")
    vop(lambda e: e.tensor_add(fap(p3_gsc, 8, 2048), fap(p3_gsc, 8, 2048),
                               fap(p3_ov, 8, 2048)), [r_gsc, r_ov], [r_gsc],
        tag="gsc1")
    vop(lambda e: e.scalar_tensor_tensor(out=fap(p3_gsc, 8, 2048),
                                         in0=fap(p3_gsc, 8, 2048), scalar=1e9,
                                         in1=fap(p3_g0, 8, 2048), op0=ALU.mult,
                                         op1=ALU.add), [r_gsc, r_g0], [r_gsc],
        tag="gsc")
    vop(lambda e: e.tensor_scalar(out=fap(p3_pos, 8, 2048), in0=fap(p3_g0, 8, 2048),
                                  scalar1=fap(p3_eb + 4, 8, 1), scalar2=None,
                                  op0=ALU.min), [r_g0, r_eb], [r_pos], tag="geff")

    for tt in range(16):
        lg = fap(lg_sb, 128, 128)[:, tt * 8:(tt + 1) * 8]
        mb = 32 + tt * 4
        vop(lambda e, lg=lg, mb=mb: e.tensor_scalar(
            out=fap(p3_mask1, 128, 8), in0=lg, scalar1=wap(mb, 1),
            scalar2=None, op0=ALU.is_equal), [r_lg, r_m], [r_mask1], tag=f"s_k1_{tt}")
        vop(lambda e, lg=lg, mb=mb: e.scalar_tensor_tensor(
            out=fap(p3_lg2, 128, 8), in0=fap(p3_mask1, 128, 8), scalar=-1e30,
            in1=lg, op0=ALU.mult, op1=ALU.add), [r_lg, r_mask1], [r_lg2],
            tag=f"s_l2_{tt}")
        vop(lambda e, mb=mb: e.tensor_scalar(
            out=fap(p3_mask2, 128, 8), in0=fap(p3_lg2, 128, 8),
            scalar1=wap(mb + 1, 1), scalar2=None, op0=ALU.is_equal),
            [r_lg2, r_m], [r_mask2], tag=f"s_k2_{tt}")
        kb.op("tensor", lambda e, tt=tt: e.transpose(
            psB[0:128, 1792:1800], fap(p3_gsc + tt * 512, 8, 128),
            fap(c_ident, 8, 8)), reads=[r_gsc, r_ident],
            writes=[RP(psB, 1792, 1800)], tag=f"s_tg_{tt}")
        kb.op("tensor", lambda e, tt=tt: e.transpose(
            psB[0:128, 1800:1808], fap(p3_pos + tt * 512, 8, 128),
            fap(c_ident, 8, 8)), reads=[r_pos, r_ident],
            writes=[RP(psB, 1800, 1808)], tag=f"s_te_{tt}")
        vop(lambda e: e.tensor_copy(fap(p3_tm, 128, 8), psB[0:128, 1792:1800]),
            [RP(psB, 1792, 1800)], [r_tm], tag=f"s_cg_{tt}")
        vop(lambda e: e.tensor_copy(fap(p3_tm + 32, 128, 8), psB[0:128, 1800:1808]),
            [RP(psB, 1800, 1808)], [r_tm], tag=f"s_ce_{tt}")
        vop(lambda e: e.tensor_copy(ix_h[:, 0:8], psB[0:128, 1792:1800]),
            [RP(psB, 1792, 1800)], [RX(0, 8)], tag=f"s_ci_{tt}")
        gop(lambda e, tt=tt: e.iota(ix_h[:, 8:9], [[1, 1]],
                                    base=3 + tt * 128, channel_multiplier=1),
            NR, [RX(8, 9)], tag=f"s_io_{tt}")
        for e_ in range(8):
            kb.op("gpsimd", lambda e, e_=e_: e.indirect_dma_start(
                out=idx_dram.ap(),
                out_offset=IOA(ix_h[:, e_:e_ + 1], 0),
                in_=ix_h[:, 8:9], in_offset=None,
                bounds_check=bnd_reg, oob_is_err=False),
                reads=[RX(0, 9)], writes=[RD(idx_dram, 0, 8192)], dma=True,
                tag=f"s_sc_{tt}_{e_}")
        vop(lambda e: e.tensor_sub(fap(p3_lg2, 128, 8),
                                   fap(p3_tm, 128, 8), fap(p3_tm + 32, 128, 8)),
            [r_tm], [r_lg2], tag=f"s_ov_{tt}")
        vop(lambda e: e.tensor_scalar(out=fap(p3_lg2, 128, 8),
                                      in0=fap(p3_lg2, 128, 8), scalar1=0.5,
                                      scalar2=None, op0=ALU.is_ge),
            [r_lg2], [r_lg2], tag=f"s_ov2_{tt}")
        for k, (rmask, moff) in enumerate(((r_mask1, p3_mask1), (r_mask2, p3_mask2))):
            vop(lambda e, moff=moff: e.tensor_mul(fap(p3_m12, 128, 8),
                                                  fap(moff, 128, 8),
                                                  fap(p3_tm + 32, 128, 8)),
                [rmask, r_tm], [r_m12], tag=f"s_sm_{tt}_{k}")
            vop(lambda e, mb=mb: e.reduce_sum(
                wap(mb + 3, 1),
                fap(p3_m12, 128, 8), axis=AX.X), [r_m12], [r_m], tag=f"s_sr_{tt}_{k}")
            vop(lambda e, tt=tt, k=k, mb=mb: e.tensor_copy(
                ix_h[:, 16 + tt * 2 + k:16 + tt * 2 + k + 1],
                wap(mb + 3, 1)), [r_m],
                [RX(16 + tt * 2 + k, 16 + tt * 2 + k + 1)],
                tag=f"s_sc2_{tt}_{k}")
            vop(lambda e, moff=moff: e.tensor_mul(fap(p3_m12, 128, 8),
                                                  fap(moff, 128, 8),
                                                  fap(p3_lg2, 128, 8)),
                [rmask, r_lg2], [r_m12], tag=f"s_wm_{tt}_{k}")
            vop(lambda e, mb=mb: e.reduce_sum(wap(mb + 3, 1),
                                              fap(p3_m12, 128, 8), axis=AX.X),
                [r_m12], [r_m], tag=f"s_wr_{tt}_{k}")
            vop(lambda e, mb=mb: e.tensor_scalar(
                out=wap(mb + 3, 1),
                in0=wap(mb + 3, 1), scalar1=-1.0, scalar2=1.0,
                op0=ALU.mult, op1=ALU.add), [r_m], [r_m], tag=f"s_wn_{tt}_{k}")
            vop(lambda e, tt=tt, k=k, mb=mb: e.tensor_mul(
                wap(tt * 2 + k, 1),
                wap(tt * 2 + k, 1),
                wap(mb + 3, 1)), [r_m, r_w12], [r_w12],
                tag=f"s_wf_{tt}_{k}")
    if DEBUG:
        dma("gpsimd", dbg["w12"].ap(), wap(0, 32), [r_w12],
            [RD(dbg["w12"], 0, 128)])
        dma("gpsimd", dbg["slots"].ap(), ix_h[:, 16:48], [RX(16, 48)],
            [RD(dbg["slots"], 0, 128)])
    A.release()


    sinb = A.take(HID * 2)
    r_sinb = RA(sinb, HID * 2)
    A.mark()
    p4_xhb = [A.take(4224 * 2) for _ in range(2)]
    p4_bc = A.take(8192 * 4)
    p4_gt = A.take(128 * 2)
    p4_mt0 = A.take(128 * 2)
    p4_t1m = [A.take(512 * 4) for _ in range(2)]
    p4_eo = [A.take(512 * 2) for _ in range(2)]
    p4_mt = [A.take(1024 * 2) for _ in range(2)]
    p4_cb = A.take(4096 * 2)
    p4_eyx = [A.take(512 * 4) for _ in range(2)]
    p4_t1f = A.take(512 * 4)
    p4_xd = A.take(4096 * 2)
    p4_y = A.take(4096 * 4)
    r_bc = RA(p4_bc, 32768)
    r_gt = RA(p4_gt, 256)
    r_mt0 = RA(p4_mt0, 256)
    r_cb = RA(p4_cb, 8192)
    r_t1f = RA(p4_t1f, 2048)
    r_xd = RA(p4_xd, 8192)
    r_y = RA(p4_y, 16384)

    vop(lambda e: e.memset(fap(carry, 128, 4096), 0.0), NR, [r_carry], tag="cz")

    for c in range(NCHUNK):
        xq = p4_xhb[c % 2]
        r_xq = RA(xq, 8448)
        dmaT("scalar", bap(xq, 128, 4224),
             conv_dump.ap()[0:4224, c * 128:(c + 1) * 128],
             [RD(conv_dump, 0, 4224)], [r_xq], tag=f"c{c}_xh")
        dma("gpsimd", fap(p4_bc, 128, 8192),
            bass.AP(rowdump, c * 8192, [[0, 128], [1, 8192]]),
            [RD(rowdump, c, c + 1)], [r_bc], tag=f"c{c}_bc")
        mmg([(psB[0:128, 0:128], bap(bfm, 128, 2048)[:, c * 128:(c + 1) * 128],
              bap(cfm, 128, 2048)[:, c * 128:(c + 1) * 128], True, True)],
            [r_bfm, r_cfm], [RP(psB, 0, 128)], tag=f"c{c}_gt")
        vop(lambda e: e.tensor_mul(bap(p4_gt, 128, 128), psB[0:128, 0:128],
                                   fap(c_trib, 128, 128)),
            [RP(psB, 0, 128), r_trib], [r_gt], tag=f"c{c}_gtm")
        vop(lambda e: e.tensor_copy(bap(p4_cb, 128, 4096), fap(carry, 128, 4096)),
            [r_carry], [r_cb], tag=f"c{c}_cb")
        def build_mt_quad(q):
            t1 = p4_t1m[q % 2]
            eo = p4_eo[q % 2]
            r_t1, r_eo = RA(t1, 2048), RA(eo, 1024)
            for j in range(4):
                h = q * 4 + j
                vop(lambda e, t1=t1, j=j, h=h, c=c: e.tensor_scalar(
                    out=fap(t1 + j * 512, 128, 128),
                    in0=fap(p4_bc + h * 512, 128, 128),
                    scalar1=fap(dl_t + (c * 64 + h) * 4, 128, 1),
                    scalar2=fap(lndt_t + (c * 64 + h) * 4, 128, 1),
                    op0=ALU.subtract, op1=ALU.min),
                    [r_bc, r_dlt, r_lndtt], [r_t1], tag=f"c{c}_t1_{h}")
            sop(lambda e, t1=t1, eo=eo: e.activation(out=bap(eo, 128, 512),
                                                     in_=fap(t1, 128, 512),
                                                     func=AF.Exp),
                [r_t1], [r_eo], tag=f"c{c}_ex{q}")
            for j in range(4):
                h = q * 4 + j
                g = h // 8
                sl = h % 8
                mtb = p4_mt[g % 2]
                r_mtb = RA(mtb, 2048)
                vop(lambda e, eo=eo, j=j: e.tensor_mul(
                    bap(p4_mt0, 128, 128), bap(eo, 128, 512)[:, j * 128:(j + 1) * 128],
                    bap(p4_gt, 128, 128)), [r_eo, r_gt], [r_mt0], tag=f"c{c}_mt0_{h}")
                vop(lambda e, mtb=mtb, sl=sl, h=h: e.scalar_tensor_tensor(
                    out=bap(mtb, 128, 1024)[:, sl * 128:(sl + 1) * 128],
                    in0=bap(c_identb, 128, 128),
                    scalar=fap(c_dcol + h * 4, 128, 1),
                    in1=bap(p4_mt0, 128, 128), op0=ALU.mult, op1=ALU.add),
                    [r_identb, r_dcol, r_mt0], [r_mtb], tag=f"c{c}_mtd_{h}")

        for g in range(8):
            build_mt_quad(2 * g)
            build_mt_quad(2 * g + 1)
            mtb = p4_mt[g % 2]
            r_mtb = RA(mtb, 2048)
            cd0 = (g % 2) * 512
            co0 = 1024 + (g % 2) * 512
            if c > 0:
                mmg([(psA[0:128, co0:co0 + 512],
                      bap(cfm, 128, 2048)[:, c * 128:(c + 1) * 128],
                      bap(p4_cb, 128, 4096)[:, g * 512:(g + 1) * 512], True, True)],
                    [r_cfm, r_cb], [RP(psA, co0, co0 + 512)], tag=f"c{c}_yo{g}")
            items = []
            for hh in range(8):
                h = g * 8 + hh
                items.append((psA[0:128, cd0 + hh * 64:cd0 + hh * 64 + 64],
                              bap(mtb, 128, 1024)[:, hh * 128:(hh + 1) * 128],
                              bap(xq, 128, 4224)[:, h * 64:(h + 1) * 64],
                              True, True))
            mmg(items, [r_mtb, r_xq], [RP(psA, cd0, cd0 + 512)], tag=f"c{c}_yd{g}")
            ex0 = 512 + (g % 2) * 512
            mmg([(psB[0:128, ex0:ex0 + 512],
                  bap(ey_fmb, 64, 2048)[:, c * 128:(c + 1) * 128],
                  bap(c_rsel, 64, HID)[:, g * 512:(g + 1) * 512], True, True)],
                [r_eyfmb, r_rsel], [RP(psB, ex0, ex0 + 512)], tag=f"c{c}_ey{g}")
            eyx = p4_eyx[g % 2]
            r_eyx = RA(eyx, 2048)
            vop(lambda e, eyx=eyx, ex0=ex0: e.tensor_copy(
                fap(eyx, 128, 512), psB[0:128, ex0:ex0 + 512]),
                [RP(psB, ex0, ex0 + 512)], [r_eyx], tag=f"c{c}_eyc{g}")
            if c > 0:
                vop(lambda e, eyx=eyx, co0=co0: e.tensor_mul(
                    fap(p4_t1f, 128, 512), psA[0:128, co0:co0 + 512],
                    fap(eyx, 128, 512)),
                    [RP(psA, co0, co0 + 512), r_eyx], [r_t1f], tag=f"c{c}_ym{g}")
                vop(lambda e, g=g, cd0=cd0: e.tensor_add(
                    fap(p4_y + g * 2048, 128, 512), fap(p4_t1f, 128, 512),
                    psA[0:128, cd0:cd0 + 512]),
                    [r_t1f, RP(psA, cd0, cd0 + 512)], [r_y], tag=f"c{c}_ya{g}")
            else:
                vop(lambda e, g=g, cd0=cd0: e.tensor_copy(
                    fap(p4_y + g * 2048, 128, 512), psA[0:128, cd0:cd0 + 512]),
                    [RP(psA, cd0, cd0 + 512)], [r_y], tag=f"c{c}_ya{g}")
        dma("sync", ylocal_dram.ap()[c * 128:(c + 1) * 128, :],
            fap(p4_y, 128, 4096), [r_y], [RD(ylocal_dram, c * 128, (c + 1) * 128)],
            tag=f"c{c}_yst")
        for h in range(HEADS):
            vop(lambda e, h=h, c=c, xq=xq: e.tensor_scalar(
                out=bap(p4_xd, 128, 4096)[:, h * 64:(h + 1) * 64],
                in0=bap(xq, 128, 4224)[:, h * 64:(h + 1) * 64],
                scalar1=fap(dtds_t + (c * 64 + h) * 4, 128, 1),
                scalar2=None, op0=ALU.mult),
                [r_xq, r_dtdst], [r_xd], tag=f"c{c}_xd{h}")
        for g in range(8):
            c0 = (g % 4) * 512
            mmg([(psA[0:128, c0:c0 + 512], bap(xq, 128, 4224)[:, 4096:4224],
                  bap(p4_xd, 128, 4096)[:, g * 512:(g + 1) * 512], True, True)],
                [r_xq, r_xd], [RP(psA, c0, c0 + 512)], tag=f"c{c}_st{g}")
            for hh in range(8):
                h = g * 8 + hh
                vop(lambda e, h=h, c0=c0, hh=hh, c=c: e.scalar_tensor_tensor(
                    out=fap(carry + h * 256, 128, 64),
                    in0=fap(carry + h * 256, 128, 64),
                    scalar=fap(cd_bc + (h * 16 + c) * 4, 128, 1),
                    in1=psA[0:128, c0 + hh * 64:c0 + hh * 64 + 64],
                    op0=ALU.mult, op1=ALU.add),
                    [RA(carry + h * 256, 256), r_cdbc,
                     RP(psA, c0 + hh * 64, c0 + hh * 64 + 64)],
                    [RA(carry + h * 256, 256)], tag=f"c{c}_cu{h}")
    if DEBUG:
        dma("gpsimd", dbg["carry"].ap(), fap(carry, 128, 4096), [r_carry],
            [RD(dbg["carry"], 0, 128)])

    vop(lambda e: e.tensor_scalar(out=fap(carry, 128, 4096),
                                  in0=fap(carry, 128, 4096),
                                  scalar1=fap(c_gates, 128, 1), scalar2=None,
                                  op0=ALU.mult), [r_carry, r_gates], [r_carry],
        tag="smask")
    dma("sync", cc_in.ap(), fap(carry, 128, 4096), [r_carry],
        [RD(cc_in, 0, 128)], tag="ccw")
    kb.op("gpsimd", lambda e: e.collective_compute(
        "AllReduce", ALU.add, replica_groups=[[0, 1], [2, 3], [4, 5], [6, 7]],
        ins=[cc_in.ap().opt()], outs=[cc_out.ap().opt()]),
        reads=[RD(cc_in, 0, 128)], writes=[RD(cc_out, 0, 128)], inc=("cc", 1),
        tag="ccall")
    dma("sync", fap(carry, 128, 4096), cc_out.ap(), [RD(cc_out, 0, 128)],
        [r_carry], tag="ccr")
    vop(lambda e: e.tensor_scalar(out=bap(sinb, 128, 4096),
                                  in0=fap(carry, 128, 4096),
                                  scalar1=fap(c_gates + 4, 128, 1), scalar2=None,
                                  op0=ALU.mult), [r_carry, r_gates], [r_sinb],
        tag="srecv")
    A.release()

    A.mark()
    p6_u = A.take(16384 * 2)
    p6_xf = A.take(8192 * 2)
    p6_it = A.take(4 * 2048 * 2)
    p6_s0 = A.take(2048 * 4)
    p6_gx = [A.take(2048 * 2) for _ in range(2)]
    p6_ix = A.take(32 * 4)
    p6_oe = [A.take(2048 * 4) for _ in range(2)]
    r_u0, r_u1 = RA(p6_u, 16384), RA(p6_u + 16384, 16384)
    r_xf = RA(p6_xf, 16384)
    r_it = RA(p6_it, 16384)
    r_s0 = RA(p6_s0, 8192)
    r_ix = RA(p6_ix, 128)

    for ex in range(NEXP):
        for g in range(8):
            dma("sync", ix_h[:, 48 + g:49 + g],
                bass.AP(idx_dram, ex * CAP + g * 128, [[1, 128], [1, 1]]),
                [RD(idx_dram, ex * CAP + g * 128, ex * CAP + g * 128 + 128)],
                [RX(48 + g, 49 + g)], tag=f"f{ex}_ix{g}")
        for g in range(8):
            gx = p6_gx[g % 2]
            r_gx = RA(gx, 4096)
            kb.op("gpsimd", lambda e, gx=gx, g=g: e.indirect_dma_start(
                out=bap(gx, 128, 2048), out_offset=None,
                in_=xn_dram.ap(), in_offset=IOA(ix_h[:, 48 + g:49 + g], 0)),
                reads=[RX(48 + g, 49 + g), RD(xn_dram, 0, 2064)], writes=[r_gx],
                dma=True, tag=f"f{ex}_g{g}")
            dma("gpsimd", xsorted.ap()[ex * CAP + g * 128:ex * CAP + (g + 1) * 128, :],
                bap(gx, 128, 2048), [r_gx],
                [RD(xsorted, ex * CAP + g * 128, ex * CAP + (g + 1) * 128)],
                tag=f"f{ex}_xs{g}")
        for mh in range(2):
            dma("gpsimd", bap(p6_u, 128, 8192),
                bass.AP(wi0, ex * DIM * MM + mh * 512,
                        [[MM, 128], [128 * MM, 16], [1, 512]]), NR, [r_u0],
                tag=f"f{ex}_w0{mh}")
            dma("gpsimd", bap(p6_u + 16384, 128, 8192),
                bass.AP(wi1, ex * DIM * MM + mh * 512,
                        [[MM, 128], [128 * MM, 16], [1, 512]]), NR, [r_u1],
                tag=f"f{ex}_w1{mh}")
            for cg in range(2):
                for kc in range(16):
                    dmaT("scalar", bap(p6_xf + kc * 1024, 128, 512),
                         xsorted.ap()[ex * CAP + cg * 512:ex * CAP + (cg + 1) * 512,
                                      kc * 128:(kc + 1) * 128],
                         [RD(xsorted, ex * CAP + cg * 512, ex * CAP + (cg + 1) * 512)],
                         [RA(p6_xf + kc * 1024, 1024)], tag=f"f{ex}_xf{mh}_{cg}_{kc}")
                for kc in range(16):
                    items = []
                    for mt in range(4):
                        items.append((psA[0:128, mt * 512:(mt + 1) * 512],
                                      bap(p6_u, 128, 8192)[:, kc * 512 + mt * 128:
                                                           kc * 512 + (mt + 1) * 128],
                                      bap(p6_xf + kc * 1024, 128, 512),
                                      kc == 0, kc == 15))
                    for mt in range(4):
                        items.append((psB[0:128, mt * 512:(mt + 1) * 512],
                                      bap(p6_u + 16384, 128, 8192)[:, kc * 512 + mt * 128:
                                                                   kc * 512 + (mt + 1) * 128],
                                      bap(p6_xf + kc * 1024, 128, 512),
                                      kc == 0, kc == 15))
                    mmg(items, [r_u0, r_u1, RA(p6_xf + kc * 1024, 1024)],
                        [RP(psA, 0, 2048), RP(psB, 0, 2048)], tag=f"f{ex}_h{mh}_{cg}_{kc}")
                sop(lambda e: e.activation(out=fap(p6_s0, 128, 2048),
                                           in_=psA[0:128, 0:2048], func=AF.Silu),
                    [RP(psA, 0, 2048)], [r_s0], tag=f"f{ex}_si{mh}_{cg}")
                vop(lambda e, mh=mh, cg=cg: e.tensor_mul(
                    bap(p6_it + (mh * 2 + cg) * 4096, 128, 2048),
                    fap(p6_s0, 128, 2048), psB[0:128, 0:2048]),
                    [r_s0, RP(psB, 0, 2048)], [RA(p6_it + (mh * 2 + cg) * 4096, 4096)],
                    tag=f"f{ex}_im{mh}_{cg}")
        dma("gpsimd", bap(p6_u, 128, 8192),
            bass.AP(wo, ex * MM * DIM, [[DIM, 128], [128 * DIM, 4], [1, 2048]]),
            NR, [r_u0], tag=f"f{ex}_wo0")
        dma("gpsimd", bap(p6_u + 16384, 128, 8192),
            bass.AP(wo, ex * MM * DIM + 512 * DIM,
                    [[DIM, 128], [128 * DIM, 4], [1, 2048]]),
            NR, [r_u1], tag=f"f{ex}_wo1")
        for ct in range(8):
            ps = psA if ct % 2 == 0 else psB
            rps = RP(ps, 0, 2048)
            cg = ct // 4
            for half in range(2):
                items = []
                for mcl in range(4):
                    mc = half * 4 + mcl
                    mhh, mtt = mc // 4, mc % 4
                    lhsT = bap(p6_it + (mhh * 2 + cg) * 4096, 128, 2048)[
                        :, mtt * 512 + (ct % 4) * 128: mtt * 512 + (ct % 4) * 128 + 128]
                    uoff = p6_u + half * 16384
                    for dg in range(4):
                        items.append((ps[0:128, dg * 512:(dg + 1) * 512], lhsT,
                                      bap(uoff, 128, 8192)[:, mcl * 2048 + dg * 512:
                                                           mcl * 2048 + (dg + 1) * 512],
                                      mc == 0, mc == 7))
                mmg(items, [r_it, (r_u0 if half == 0 else r_u1)], [rps],
                    tag=f"f{ex}_o{ct}_{half}")
            oe = p6_oe[ct % 2]
            r_oe = RA(oe, 8192)
            vop(lambda e, ps=ps, oe=oe: e.tensor_copy(fap(oe, 128, 2048), ps[0:128, 0:2048]),
                [rps], [r_oe], tag=f"f{ex}_oe{ct}")
            dma("sync", outsorted.ap()[ex * CAP + ct * 128:ex * CAP + (ct + 1) * 128, :],
                fap(oe, 128, 2048), [r_oe],
                [RD(outsorted, ex * CAP + ct * 128, ex * CAP + (ct + 1) * 128)],
                tag=f"f{ex}_os{ct}")
    A.release()

    A.mark()
    p7_y = A.take(4096 * 4)
    p7_zs = A.take(4096 * 4)
    p7_zb = A.take(4096 * 2)
    p7_ecb = A.take(128 * 2)
    p7_ex = [A.take(512 * 4) for _ in range(2)]
    p7_t1 = A.take(512 * 4)
    p7_sq = A.take(4096 * 4)
    p7_st = A.take(16)
    p7_yn = A.take(4096 * 2)
    r_7y = RA(p7_y, 16384)
    r_7zs = RA(p7_zs, 16384)
    r_7zb = RA(p7_zb, 8192)
    r_7ecb = RA(p7_ecb, 256, 0, 64)
    r_7t1 = RA(p7_t1, 2048)
    r_7sq = RA(p7_sq, 16384)
    r_7st = RA(p7_st, 16)
    r_7yn = RA(p7_yn, 8192)

    for c in range(NCHUNK):
        dma("sync", fap(p7_y, 128, 4096), ylocal_dram.ap()[c * 128:(c + 1) * 128, :],
            [RD(ylocal_dram, c * 128, (c + 1) * 128)], [r_7y], tag=f"p7y{c}")
        dma("sync", bap(p7_zb, 128, 4096), z_dram.ap()[c * 128:(c + 1) * 128, :],
            [RD(z_dram, c * 128, (c + 1) * 128)], [r_7zb], tag=f"p7z{c}")
        sop(lambda e: e.activation(out=fap(p7_zs, 128, 4096),
                                   in_=bap(p7_zb, 128, 4096), func=AF.Silu),
            [r_7zb], [r_7zs], tag=f"p7s{c}")
        vop(lambda e, c=c: e.tensor_scalar(
            out=bap(p7_ecb, 64, 128),
            in0=bap(ey_fmb, 64, 2048)[:, c * 128:(c + 1) * 128],
            scalar1=fap(cdp_fm + c * 4, 64, 1), scalar2=None, op0=ALU.mult),
            [r_eyfmb, r_cdpfm], [r_7ecb], tag=f"p7e{c}")
        for g in range(8):
            c0 = (g % 2) * 512
            e0 = 1024 + (g % 2) * 512
            mmg([(psB[0:128, c0:c0 + 512],
                  bap(cfm, 128, 2048)[:, c * 128:(c + 1) * 128],
                  bap(sinb, 128, 4096)[:, g * 512:(g + 1) * 512], True, True)],
                [r_cfm, r_sinb], [RP(psB, c0, c0 + 512)], tag=f"p7c{c}_{g}")
            mmg([(psB[0:128, e0:e0 + 512], bap(p7_ecb, 64, 128),
                  bap(c_rsel, 64, HID)[:, g * 512:(g + 1) * 512], True, True)],
                [r_7ecb, r_rsel], [RP(psB, e0, e0 + 512)], tag=f"p7x{c}_{g}")
            ex_ = p7_ex[g % 2]
            r_ex = RA(ex_, 2048)
            vop(lambda e, ex_=ex_, e0=e0: e.tensor_copy(fap(ex_, 128, 512),
                                                        psB[0:128, e0:e0 + 512]),
                [RP(psB, e0, e0 + 512)], [r_ex], tag=f"p7xc{c}_{g}")
            vop(lambda e, ex_=ex_, c0=c0: e.tensor_mul(
                fap(p7_t1, 128, 512), psB[0:128, c0:c0 + 512], fap(ex_, 128, 512)),
                [RP(psB, c0, c0 + 512), r_ex], [r_7t1], tag=f"p7m{c}_{g}")
            vop(lambda e, g=g: e.tensor_add(fap(p7_y + g * 2048, 128, 512),
                                            fap(p7_y + g * 2048, 128, 512),
                                            fap(p7_t1, 128, 512)),
                [r_7t1, r_7y], [r_7y], tag=f"p7a{c}_{g}")
        vop(lambda e: e.tensor_mul(fap(p7_y, 128, 4096), fap(p7_y, 128, 4096),
                                   fap(p7_zs, 128, 4096)), [r_7y, r_7zs], [r_7y],
            tag=f"p7g{c}")
        sop(lambda e: e.activation(out=fap(p7_sq, 128, 4096),
                                   in_=fap(p7_y, 128, 4096), func=AF.Square,
                                   accum_out=fap(p7_st, 128, 1)),
            [r_7y], [r_7sq, r_7st], tag=f"p7q{c}")
        vop(lambda e: e.tensor_scalar(out=fap(p7_st + 4, 128, 1),
                                      in0=fap(p7_st, 128, 1), scalar1=1.0 / 4096,
                                      scalar2=1e-5, op0=ALU.mult, op1=ALU.add),
            [r_7st], [r_7st], tag=f"p7v{c}")
        sop(lambda e: e.activation(out=fap(p7_st + 8, 128, 1),
                                   in_=fap(p7_st + 4, 128, 1), func=AF.Ln),
            [r_7st], [r_7st], tag=f"p7l{c}")
        sop(lambda e: e.activation(out=fap(p7_st + 8, 128, 1),
                                   in_=fap(p7_st + 8, 128, 1), func=AF.Exp,
                                   scale=-0.5), [r_7st], [r_7st], tag=f"p7r{c}")
        vop(lambda e: e.tensor_scalar(out=bap(p7_yn, 128, 4096),
                                      in0=fap(p7_y, 128, 4096),
                                      scalar1=fap(p7_st + 8, 128, 1), scalar2=None,
                                      op0=ALU.mult), [r_7y, r_7st], [r_7yn],
            tag=f"p7n{c}")
        dma("sync", ynorm_dram.ap()[c * 128:(c + 1) * 128, :], bap(p7_yn, 128, 4096),
            [r_7yn], [RD(ynorm_dram, c * 128, (c + 1) * 128)], tag=f"p7st{c}")
    A.release()

    A.mark()
    p8_yf = [A.take(4096 * 2) for _ in range(2)]
    p8_ws = A.take(8 * 2048 * 2)
    p8_ev = [A.take(2048 * 4) for _ in range(2)]
    r_ws = RA(p8_ws, 32768)

    for hcg in range(4):
        for h8 in range(8):
            hc = hcg * 8 + h8
            dma("gpsimd", bap(p8_ws + h8 * 4096, 128, 2048),
                bass.AP(woutT, hc * 128 * DIM, [[DIM, 128], [1, 2048]]),
                NR, [RA(p8_ws + h8 * 4096, 4096)], tag=f"s{hcg}_w{h8}")
            vop(lambda e, h8=h8, hc=hc: e.tensor_scalar(
                out=bap(p8_ws + h8 * 4096, 128, 2048),
                in0=bap(p8_ws + h8 * 4096, 128, 2048),
                scalar1=fap(c_nw + hc * 4, 128, 1), scalar2=None, op0=ALU.mult),
                [RA(p8_ws + h8 * 4096, 4096), r_nw], [RA(p8_ws + h8 * 4096, 4096)],
                tag=f"s{hcg}_n{h8}")
        for tg in range(4):
            yf = p8_yf[tg % 2]
            r_yf = RA(yf, 8192)
            for h8 in range(8):
                hc = hcg * 8 + h8
                dmaT("scalar", bap(yf + h8 * 1024, 128, 512),
                     ynorm_dram.ap()[tg * 512:(tg + 1) * 512, hc * 128:(hc + 1) * 128],
                     [RD(ynorm_dram, tg * 512, (tg + 1) * 512)],
                     [RA(yf + h8 * 1024, 1024)], tag=f"s{hcg}_y{tg}_{h8}")
            for tile in range(4):
                tt = tg * 4 + tile
                ps = psA if tile % 2 == 0 else psB
                rps = RP(ps, 0, 2048)
                items = []
                for h8 in range(8):
                    for dg in range(4):
                        items.append((ps[0:128, dg * 512:(dg + 1) * 512],
                                      bap(yf + h8 * 1024, 128, 512)[
                                          :, tile * 128:(tile + 1) * 128],
                                      bap(p8_ws + h8 * 4096, 128, 2048)[
                                          :, dg * 512:(dg + 1) * 512],
                                      h8 == 0, h8 == 7))
                mmg(items, [r_yf, r_ws], [rps], tag=f"s{hcg}_m{tt}")
                ev = p8_ev[tile % 2]
                r_ev = RA(ev, 8192)
                vop(lambda e, ev=ev, ps=ps: e.tensor_copy(fap(ev, 128, 2048),
                                                          ps[0:128, 0:2048]),
                    [rps], [r_ev], tag=f"s{hcg}_e{tt}")
                if hcg == 0:
                    dma("gpsimd", s_dram.ap()[tt * 128:(tt + 1) * 128, :],
                        fap(ev, 128, 2048), [r_ev],
                        [RD(s_dram, tt * 128, (tt + 1) * 128)], tag=f"s{hcg}_s{tt}")
                else:
                    dma("gpsimd", s_dram.ap()[tt * 128:(tt + 1) * 128, :],
                        fap(ev, 128, 2048), [r_ev],
                        [RD(s_dram, tt * 128, (tt + 1) * 128)],
                        accum_op=ALU.add, tag=f"s{hcg}_s{tt}")
    A.release()

    A.mark()
    p9_s = [A.take(2048 * 4) for _ in range(2)]
    p9_g1 = [A.take(2048 * 4) for _ in range(2)]
    p9_g2 = [A.take(2048 * 4) for _ in range(2)]
    p9_x = [A.take(2048 * 4) for _ in range(2)]
    p9_gg = [A.take(2048 * 4) for _ in range(2)]
    for tt in range(16):
        bs, bg1, bg2, bx, bgg = (p9_s[tt % 2], p9_g1[tt % 2], p9_g2[tt % 2],
                                 p9_x[tt % 2], p9_gg[tt % 2])
        r_s = RA(bs, 8192)
        r_g1, r_g2 = RA(bg1, 8192), RA(bg2, 8192)
        r_x, r_gg = RA(bx, 8192), RA(bgg, 8192)
        dma("sync", fap(bs, 128, 2048), s_dram.ap()[tt * 128:(tt + 1) * 128, :],
            [RD(s_dram, tt * 128, (tt + 1) * 128)], [r_s], tag=f"o{tt}_s")
        kb.op("gpsimd", lambda e, bg1=bg1, tt=tt: e.indirect_dma_start(
            out=fap(bg1, 128, 2048), out_offset=None, in_=outsorted.ap(),
            in_offset=IOA(ix_h[:, 16 + tt * 2:17 + tt * 2], 0)),
            reads=[RX(16 + tt * 2, 17 + tt * 2), RD(outsorted, 0, 8192)],
            writes=[r_g1], dma=True, tag=f"o{tt}_g1")
        kb.op("gpsimd", lambda e, bg2=bg2, tt=tt: e.indirect_dma_start(
            out=fap(bg2, 128, 2048), out_offset=None, in_=outsorted.ap(),
            in_offset=IOA(ix_h[:, 17 + tt * 2:18 + tt * 2], 0)),
            reads=[RX(17 + tt * 2, 18 + tt * 2), RD(outsorted, 0, 8192)],
            writes=[r_g2], dma=True, tag=f"o{tt}_g2")
        dma("sync", fap(bx, 128, 2048), x_in.ap()[3 + tt * 128:3 + (tt + 1) * 128, :],
            NR, [r_x], tag=f"o{tt}_x")
        vop(lambda e, bgg=bgg, bg1=bg1, tt=tt: e.tensor_scalar(
            out=fap(bgg, 128, 2048), in0=fap(bg1, 128, 2048),
            scalar1=wap(tt * 2, 1), scalar2=None, op0=ALU.mult),
            [r_g1, r_w12], [r_gg], tag=f"o{tt}_m1")
        vop(lambda e, bgg=bgg, bg2=bg2, tt=tt: e.scalar_tensor_tensor(
            out=fap(bgg, 128, 2048), in0=fap(bg2, 128, 2048),
            scalar=wap(tt * 2 + 1, 1), in1=fap(bgg, 128, 2048),
            op0=ALU.mult, op1=ALU.add), [r_g2, r_w12, r_gg], [r_gg],
            tag=f"o{tt}_m2")
        if DEBUG:
            dma("gpsimd", dbg["gsum"].ap()[tt * 128:(tt + 1) * 128, :],
                fap(bgg, 128, 2048), [r_gg],
                [RD(dbg["gsum"], tt * 128, (tt + 1) * 128)])
            dma("gpsimd", dbg["sproj"].ap()[tt * 128:(tt + 1) * 128, :],
                fap(bs, 128, 2048), [r_s],
                [RD(dbg["sproj"], tt * 128, (tt + 1) * 128)])
        sop(lambda e, bgg=bgg, bg1=bg1: e.activation(
            out=fap(bg1, 128, 2048), in_=fap(bgg, 128, 2048), func=AF.Silu),
            [r_gg], [r_g1], tag=f"o{tt}_si")
        vop(lambda e, bs=bs, bg1=bg1: e.tensor_mul(
            fap(bs, 128, 2048), fap(bs, 128, 2048), fap(bg1, 128, 2048)),
            [r_s, r_g1], [r_s], tag=f"o{tt}_sg")
        vop(lambda e, bs=bs, bx=bx: e.tensor_add(
            fap(bs, 128, 2048), fap(bs, 128, 2048), fap(bx, 128, 2048)),
            [r_s, r_x], [r_s], tag=f"o{tt}_fx")
        dma("sync", out_t.ap()[tt * 128:(tt + 1) * 128, :], fap(bs, 128, 2048),
            [r_s], [RD(out_t, tt * 128, (tt + 1) * 128)], tag=f"o{tt}_out")
    A.release()

    if DEBUG:
        A.mark()
        db = A.take(4096 * 4)
        r_db = RA(db, 16384)
        def dump(dst, src_t, rows, cols, dt_b):
            for i in range(rows // 128):
                n = cols
                if dt_b:
                    dma("sync", bap(db, 128, n), src_t.ap()[i * 128:(i + 1) * 128, :],
                        [RD(src_t, i * 128, (i + 1) * 128)], [r_db])
                    dma("sync", dst.ap()[i * 128:(i + 1) * 128, :], bap(db, 128, n),
                        [r_db], [RD(dst, i * 128, (i + 1) * 128)])
                else:
                    dma("sync", fap(db, 128, n), src_t.ap()[i * 128:(i + 1) * 128, :],
                        [RD(src_t, i * 128, (i + 1) * 128)], [r_db])
                    dma("sync", dst.ap()[i * 128:(i + 1) * 128, :], fap(db, 128, n),
                        [r_db], [RD(dst, i * 128, (i + 1) * 128)])
        dma("gpsimd", dbg["idx"].ap(), idx_dram.ap(), [RD(idx_dram, 0, 8192)],
            [RD(dbg["idx"], 0, 8192)])
        dbg["xn3"] = DO("dbg_xn3", [2048, DIM], BF16)
        dump(dbg["xn3"], xn_dram, 2048, DIM, True)
        dump(dbg["xbc"], conv_dump, CONV_DIM, T, True)
        dump(dbg["z"], z_dram, T, HID, True)
        dump(dbg["ylocal"], ylocal_dram, T, HID, False)
        dump(dbg["ynorm"], ynorm_dram, T, HID, True)
        dump(dbg["xs"], xsorted, NEXP * CAP, DIM, True)
        dump(dbg["os"], outsorted, NEXP * CAP, DIM, False)
        A.release()
    kb.barrier("sync", {k: v for k, v in kb.counts.items() if k.startswith("d_")})
    kb.barrier("gpsimd", {k: v for k, v in kb.counts.items() if k.startswith("d_")})


    kb.finalize(es)
    return nc, es, dbg


def kernel(L0_attn_x, L0_attn_in_proj_w, L0_attn_conv_w, L0_attn_dt_bias,
           L0_attn_A_log, L0_attn_D, L0_attn_norm_w, L0_attn_out_proj_w,
           L0_ffn_w_gate, L0_ffn_wi_0, L0_ffn_wi_1, L0_ffn_wo):
    x = np.asarray(L0_attn_x)
    winT = np.ascontiguousarray(np.asarray(L0_attn_in_proj_w).T)
    woutT = np.ascontiguousarray(np.asarray(L0_attn_out_proj_w).T)
    ident = np.eye(128, dtype=np.float32)
    tri = np.triu(np.ones((128, 128), dtype=np.float32))
    rsel = np.zeros((64, HID), dtype=np.float32)
    for h in range(64):
        rsel[h, h * 64:(h + 1) * 64] = 1.0
    rsel = rsel.astype(ml_dtypes.bfloat16)

    nc, es, dbg = build_core_program()
    in_maps = []
    for core in range(8):
        b, half = core // 2, core % 2
        halo = (np.zeros((3, DIM), np.float32) if half == 0
                else np.ascontiguousarray(x[b, 2045:2048]))
        own = np.ascontiguousarray(x[b, half * T:(half + 1) * T])
        in_maps.append(dict(
            x2051=np.concatenate([halo, own], axis=0),
            winT=winT, woutT=woutT,
            convw=np.ascontiguousarray(L0_attn_conv_w),
            dt_bias=np.asarray(L0_attn_dt_bias), a_log=np.asarray(L0_attn_A_log),
            d_vec=np.asarray(L0_attn_D),
            norm_w=np.ascontiguousarray(np.asarray(L0_attn_norm_w).reshape(32, 128).T),
            w_gate=np.asarray(L0_ffn_w_gate),
            wi0=np.asarray(L0_ffn_wi_0), wi1=np.asarray(L0_ffn_wi_1),
            wo=np.asarray(L0_ffn_wo),
            ident=ident, tri=tri, rsel=rsel,
            gates=np.array([1.0 - half, float(half)], np.float32),
        ))
    res = run_bass_kernel_spmd(nc, in_maps, core_ids=list(range(8)))
    es.close()
    outs = [np.asarray(r["out"]) for r in res.results]
    full = np.empty((4, 4096, DIM), np.float32)
    for core in range(8):
        b, half = core // 2, core % 2
        full[b, half * T:(half + 1) * T] = outs[core]
    if DEBUG:
        kernel.last_results = res
    return full


# version 08
# speedup vs baseline: 1.0619x; 1.0619x over previous
import numpy as np
import ml_dtypes
import concourse.bass as bass
import concourse.mybir as mybir
from concourse.bass_utils import run_bass_kernel_spmd
from contextlib import ExitStack

F32 = mybir.dt.float32
BF16 = mybir.dt.bfloat16
I32 = mybir.dt.int32
AF = mybir.ActivationFunctionType
ALU = mybir.AluOpType
AX = mybir.AxisListType
IOA = bass.IndirectOffsetOnAxis

DIM = 2048
HID = 4096
STATE = 128
HEADS = 64
HEAD_DIM = 64
CONV_DIM = 4352
D_IN = 8512
T = 2048
TH = T + 3
CH = 128
NCHUNK = 16
CAP = 1024
NEXP = 8
MM = 1024

DEBUG = False

ENGINES = ("sync", "scalar", "vector", "tensor", "gpsimd")


class Reg:
    __slots__ = ("buf", "p0", "p1", "f0", "f1")

    def __init__(self, buf, p0, p1, f0, f1):
        self.buf, self.p0, self.p1, self.f0, self.f1 = buf, p0, p1, f0, f1

    def overlaps(self, o):
        return (self.buf is o.buf and self.p0 < o.p1 and o.p0 < self.p1
                and self.f0 < o.f1 and o.f0 < self.f1)


class KB:
    def __init__(self, nc):
        self.nc = nc
        self.ops = []
        self.sem_names = set()
        self.counts = {}
        self.hist = {}

    def op(self, engine, emit, reads=(), writes=(), dma=False, inc=None, tag=""):
        if inc is None:
            sem = ("d_" + engine) if dma else ("q_" + engine)
            amt = 16 if dma else 1
        else:
            sem, amt = inc
        self.sem_names.add(sem)
        cnt = self.counts.get(sem, 0) + amt
        self.counts[sem] = cnt
        waits = {}

        def need(ds, dc, de):
            if engine == "tensor" and de == "tensor":
                return
            if waits.get(ds, 0) < dc:
                waits[ds] = dc

        for r in reads:
            for (hr, kind, hs, hc, he) in self.hist.get(id(r.buf), ()):
                if kind == "w" and hr.overlaps(r):
                    need(hs, hc, he)
        for w in writes:
            for (hr, kind, hs, hc, he) in self.hist.get(id(w.buf), ()):
                if hr.overlaps(w):
                    need(hs, hc, he)
        rec = dict(engine=engine, emit=emit, waits=waits, sem=sem, amt=amt,
                   cnt=cnt, tag=tag)
        self.ops.append(rec)
        for r in reads:
            self.hist.setdefault(id(r.buf), []).append((r, "r", sem, cnt, engine))
        for w in writes:
            h = self.hist.setdefault(id(w.buf), [])
            h[:] = [e for e in h if not (w.p0 <= e[0].p0 and w.p1 >= e[0].p1
                    and w.f0 <= e[0].f0 and w.f1 >= e[0].f1)]
            h.append((w, "w", sem, cnt, engine))
        return rec

    def barrier(self, engine, waits):
        self.ops.append(dict(engine=engine, emit=None, waits=dict(waits),
                             sem=None, amt=0, cnt=0, tag="barrier"))
        for s in waits:
            self.sem_names.add(s)

    def finalize(self, es):
        sems = {n: es.enter_context(self.nc.semaphore(n))
                for n in sorted(self.sem_names)}
        streams = {e: [] for e in ENGINES}
        for rec in self.ops:
            streams[rec["engine"]].append(rec)
        with self.nc.Block() as block:
            for ename in ENGINES:
                recs = streams[ename]
                if not recs:
                    continue

                def body(eng, recs=recs):
                    water = {}
                    for rec in recs:
                        for sn, sc in sorted(rec["waits"].items()):
                            if water.get(sn, 0) < sc:
                                eng.wait_ge(sems[sn], sc)
                                water[sn] = sc
                        if rec["emit"] is not None:
                            rec["emit"](eng).then_inc(sems[rec["sem"]], rec["amt"])
                getattr(block, ename)(body)


class Arena:

    def __init__(self, nbytes):
        self.n = nbytes
        self.at = 0
        self.peak = 0
        self.marks = []

    def take(self, nbytes, align=64):
        a = (self.at + align - 1) // align * align
        assert a + nbytes <= self.n, f"SBUF arena overflow: {a}+{nbytes} > {self.n}"
        self.at = a + nbytes
        self.peak = max(self.peak, self.at)
        return a

    def mark(self):
        self.marks.append(self.at)

    def release(self):
        self.at = self.marks.pop()


def build_core_program():
    nc = bass.Bass("TRN2", target_bir_lowering=False)

    def DI(name, shape, dtype=F32):
        return nc.dram_tensor(name, shape, dtype, kind="ExternalInput")

    def DO(name, shape, dtype=F32):
        return nc.dram_tensor(name, shape, dtype, kind="ExternalOutput")

    def DS(name, shape, dtype):
        return nc.dram_tensor(name, shape, dtype)

    x_in = DI("x2051", [TH, DIM])
    winT = DI("winT", [DIM, D_IN])
    woutT = DI("woutT", [HID, DIM])
    convw = DI("convw", [CONV_DIM, 4])
    dt_bias = DI("dt_bias", [HEADS])
    a_log = DI("a_log", [HEADS])
    d_in = DI("d_vec", [HEADS])
    norm_w = DI("norm_w", [128, HID // 128])
    w_gate = DI("w_gate", [DIM, NEXP])
    wi0 = DI("wi0", [NEXP, DIM, MM])
    wi1 = DI("wi1", [NEXP, DIM, MM])
    wo = DI("wo", [NEXP, MM, DIM])
    ident_in = DI("ident", [128, 128])
    tri_in = DI("tri", [128, 128])
    rsel_in = DI("rsel", [64, HID], BF16)
    gates_in = DI("gates", [2])

    out_t = DO("out", [T, DIM])

    xn_dram = DS("xn_dram", [2064, DIM], BF16)
    conv_dump = DS("conv_dump", [CONV_DIM, T], BF16)
    z_dram = DS("z_dram", [T, HID], BF16)
    ylocal_dram = DS("ylocal_dram", [T, HID], F32)
    ynorm_dram = DS("ynorm_dram", [T, HID], BF16)
    s_dram = DS("s_dram", [T, DIM], F32)
    xsorted = DS("xsorted", [NEXP * CAP, DIM], BF16)
    outsorted = DS("outsorted", [NEXP * CAP, DIM], BF16)
    idx_dram = DS("idx_dram", [NEXP * CAP, 1], I32)
    rowdump = DS("rowdump", [NCHUNK, HEADS * CH], F32)
    smalldump = DS("smalldump", [4096], F32)
    cc_in = DS("cc_in", [STATE, HID], F32)
    cc_out = DS("cc_out", [STATE, HID], F32)

    dbg = {}
    if DEBUG:
        dbg["xn"] = DO("dbg_xn", [T, DIM], BF16)
        dbg["xbc"] = DO("dbg_xbc", [CONV_DIM, T], BF16)
        dbg["dt"] = DO("dbg_dt", [HEADS, T])
        dbg["dac"] = DO("dbg_dac", [HEADS, T])
        dbg["z"] = DO("dbg_z", [T, HID], BF16)
        dbg["logits"] = DO("dbg_logits", [T, NEXP])
        dbg["ylocal"] = DO("dbg_ylocal", [T, HID])
        dbg["ynorm"] = DO("dbg_ynorm", [T, HID], BF16)
        dbg["gsum"] = DO("dbg_gsum", [T, DIM])
        dbg["sproj"] = DO("dbg_sproj", [T, DIM])
        dbg["carry"] = DO("dbg_carry", [STATE, HID])
        dbg["w12"] = DO("dbg_w12", [128, 32])
        dbg["slots"] = DO("dbg_slots", [128, 32])
        dbg["dlt"] = DO("dbg_dlt", [128, 1024])
        dbg["lndtt"] = DO("dbg_lndtt", [128, 1024])
        dbg["eyfmb"] = DO("dbg_eyfmb", [64, 2048], BF16)
        dbg["cdbc"] = DO("dbg_cdbc", [128, 1024])
        dbg["idx"] = DO("dbg_idx", [NEXP * CAP, 1], I32)
        dbg["xs"] = DO("dbg_xs", [NEXP * CAP, DIM], BF16)
        dbg["os"] = DO("dbg_os", [NEXP * CAP, DIM], BF16)

    es = ExitStack()
    kb = KB(nc)

    ARB = 168 * 1024
    ar_h = es.enter_context(nc.sbuf_tensor("AR", [128, ARB // 4], F32))
    psA = es.enter_context(nc.psum_tensor("psA", [128, 2048], F32))
    psB = es.enter_context(nc.psum_tensor("psB", [128, 2048], F32))
    ix_h = es.enter_context(nc.sbuf_tensor("IX", [128, 128], I32))
    ws_h = es.enter_context(nc.sbuf_tensor("WS", [128, 128], F32))
    A = Arena(ARB)

    def RX(c0, c1, p0=0, p1=128):
        return Reg(ix_h, p0, p1, c0, c1)

    def RW(c0, c1, p0=0, p1=128):
        return Reg(ws_h, p0, p1, c0, c1)

    def wap(c0, n, pn=128):
        return ws_h[0:pn, c0:c0 + n]

    def fap(off, pn, n, p0=0):
        assert off % 4 == 0
        return ar_h[p0:p0 + pn, off // 4: off // 4 + n]

    def bap(off, pn, n, p0=0):
        assert off % 4 == 0 and n % 2 == 0
        return ar_h[p0:p0 + pn, off // 4: off // 4 + n // 2].bitcast(BF16)

    def iap(off, pn, n, p0=0):
        assert off % 4 == 0
        return ar_h[p0:p0 + pn, off // 4: off // 4 + n].bitcast(I32)

    def RA(off, nbytes, p0=0, p1=128):
        return Reg(ar_h, p0, p1, off, off + nbytes)

    def RP(ps, c0, c1, p0=0, p1=128):
        return Reg(ps, p0, p1, c0, c1)

    def RD(t, r0, r1, c0=0, c1=1 << 30):
        return Reg(t, r0, r1, c0, c1)

    def dma(q, out_ap, in_ap, reads, writes, tag="", **kw):
        return kb.op(q, lambda e: e.dma_start(out=out_ap, in_=in_ap, **kw),
                     reads=reads, writes=writes, dma=True, tag=tag)

    def dmaT(q, out_ap, in_ap, reads, writes, tag=""):
        return kb.op(q, lambda e: e.dma_start(out=out_ap, in_=in_ap,
                                              transpose=True),
                     reads=reads, writes=writes, dma=True, tag=tag)

    def vop(emit, reads, writes, tag=""):
        return kb.op("vector", emit, reads=reads, writes=writes, tag=tag)

    def sop(emit, reads, writes, tag=""):
        return kb.op("scalar", emit, reads=reads, writes=writes, tag=tag)

    def gop(emit, reads, writes, tag=""):
        return kb.op("gpsimd", emit, reads=reads, writes=writes, tag=tag)

    def mmg(items, reads, writes, tag=""):
        def emit(eng):
            ins = None
            for (o, l, r, st, sp) in items:
                ins = eng.matmul(o, l, r, start=st, stop=sp)
            return ins
        return kb.op("tensor", emit, reads=reads, writes=writes, tag=tag)

    def trans(out_ap, in_ap, ident_ap, reads, writes, tag=""):
        return kb.op("tensor",
                     lambda e: e.transpose(out_ap, in_ap, ident_ap),
                     reads=reads, writes=writes, tag=tag)

    NR = ()

    bnd_reg = nc.gpsimd.alloc_register("bndreg")
    kb.op("gpsimd", lambda e: e.reg_mov(bnd_reg, NEXP * CAP - 1), NR, NR,
          tag="bndmov")
    c_ident = A.take(128 * 4)
    c_identb = A.take(128 * 2)
    c_trib = A.take(128 * 4)
    c_rsel = A.take(HID * 2)
    c_convw = A.take(136 * 4)
    c_dbias = A.take(4)
    c_nA = A.take(4)
    c_dcol = A.take(64 * 4)
    c_nw = A.take(32 * 4)
    c_gates = A.take(2 * 4)
    c_wgate = A.take(128 * 2)
    c_zero = A.take(128 * 4)
    c_eps = A.take(4)
    c_ones1 = A.take(128 * 2)
    c_dpart = A.take(4)
    c_ones64 = A.take(128 * 2)
    d_exp = A.take(HID * 2)

    r_ident = RA(c_ident, 512)
    dma("sync", fap(c_ident, 128, 128), ident_in.ap(), NR, [r_ident])
    r_identb = RA(c_identb, 256)
    dma("gpsimd", bap(c_identb, 128, 128), ident_in.ap(), NR, [r_identb])
    r_trib = RA(c_trib, 512)
    dma("sync", fap(c_trib, 128, 128), tri_in.ap(), NR, [r_trib])
    r_rsel = RA(c_rsel, HID * 2, 0, 64)
    dma("sync", bap(c_rsel, 64, HID), rsel_in.ap(), NR, [r_rsel])
    r_convw = RA(c_convw, 544)
    dma("sync", fap(c_convw, 128, 136),
        bass.AP(convw, 0, [[4, 128], [512, 34], [1, 4]]), NR, [r_convw])
    r_dbias = RA(c_dbias, 4, 0, 64)
    dma("sync", fap(c_dbias, 64, 1), bass.AP(dt_bias, 0, [[1, 64], [1, 1]]),
        NR, [r_dbias])
    r_nA = RA(c_nA, 4, 0, 64)
    dma("sync", fap(c_nA, 64, 1), bass.AP(a_log, 0, [[1, 64], [1, 1]]),
        NR, [r_nA])
    sop(lambda e: e.activation(out=fap(c_nA, 64, 1), in_=fap(c_nA, 64, 1),
                               func=AF.Exp), [r_nA], [r_nA], tag="expA")
    vop(lambda e: e.tensor_scalar(out=fap(c_nA, 64, 1), in0=fap(c_nA, 64, 1),
                                  scalar1=-1.0, scalar2=None, op0=ALU.mult),
        [r_nA], [r_nA], tag="negA")
    r_dcol = RA(c_dcol, 256)
    dma("sync", fap(c_dcol, 128, 64), bass.AP(d_in, 0, [[0, 128], [1, 64]]),
        NR, [r_dcol])
    r_nw = RA(c_nw, 128)
    dma("sync", fap(c_nw, 128, 32), norm_w.ap(), NR, [r_nw])
    r_gates = RA(c_gates, 8)
    dma("sync", fap(c_gates, 128, 2), bass.AP(gates_in, 0, [[0, 128], [1, 2]]),
        NR, [r_gates])
    r_wgate = RA(c_wgate, 256)
    dma("gpsimd", bap(c_wgate, 128, 128),
        bass.AP(w_gate, 0, [[8, 128], [1024, 16], [1, 8]]), NR, [r_wgate])
    r_zero = RA(c_zero, 512)
    vop(lambda e: e.memset(fap(c_zero, 128, 128), 0.0), NR, [r_zero])
    r_eps = RA(c_eps, 4)
    vop(lambda e: e.memset(fap(c_eps, 128, 1), 1e-5), NR, [r_eps])
    r_dpart = RA(c_dpart, 4, 0, 64)
    dma("sync", fap(c_dpart, 64, 1), bass.AP(d_in, 0, [[1, 64], [1, 1]]),
        NR, [r_dpart], tag="ld_dpart")
    r_ones64 = RA(c_ones64, 256, 0, 64)
    vop(lambda e: e.memset(bap(c_ones64, 64, 128), 1.0), NR, [r_ones64],
        tag="ones64")
    r_dexp = RA(d_exp, HID * 2)

    A.mark()
    p0_x = [A.take(2048 * 4) for _ in range(2)]
    p0_sq = [A.take(2048 * 4) for _ in range(2)]
    p0_st = [A.take(8 * 4) for _ in range(2)]
    p0_xn = [A.take(2048 * 2) for _ in range(2)]

    def ln_tile(i, pn, row):
        bx, bq, bs, bn = p0_x[i % 2], p0_sq[i % 2], p0_st[i % 2], p0_xn[i % 2]
        rx, rq = RA(bx, 8192, 0, pn), RA(bq, 8192, 0, pn)
        rs, rxn = RA(bs, 32, 0, pn), RA(bn, 4096, 0, pn)
        dma("sync", fap(bx, pn, 2048), x_in.ap()[row:row + pn, :], NR, [rx])
        vop(lambda e: e.reduce_sum(fap(bs, pn, 1), fap(bx, pn, 2048), axis=AX.X),
            [rx], [rs])
        sop(lambda e: e.activation(out=fap(bq, pn, 2048), in_=fap(bx, pn, 2048),
                                   func=AF.Square, accum_out=fap(bs + 4, pn, 1)),
            [rx], [rq, rs])
        vop(lambda e: e.tensor_scalar(out=fap(bs, pn, 2), in0=fap(bs, pn, 2),
                                      scalar1=1.0 / 2048, scalar2=None,
                                      op0=ALU.mult), [rs], [rs])
        vop(lambda e: e.tensor_scalar(out=fap(bs + 8, pn, 1), in0=fap(bs, pn, 1),
                                      scalar1=fap(bs, pn, 1), scalar2=None,
                                      op0=ALU.mult), [rs], [rs])
        vop(lambda e: e.tensor_sub(fap(bs + 12, pn, 1), fap(bs + 4, pn, 1),
                                   fap(bs + 8, pn, 1)), [rs], [rs])
        sop(lambda e: e.activation(out=fap(bs + 16, pn, 1), in_=fap(bs + 12, pn, 1),
                                   func=AF.Ln, bias=fap(c_eps, pn, 1)),
            [rs, r_eps], [rs])
        sop(lambda e: e.activation(out=fap(bs + 16, pn, 1), in_=fap(bs + 16, pn, 1),
                                   func=AF.Exp, scale=-0.5), [rs], [rs])
        vop(lambda e: e.tensor_scalar(out=bap(bn, pn, 2048), in0=fap(bx, pn, 2048),
                                      scalar1=fap(bs, pn, 1),
                                      scalar2=fap(bs + 16, pn, 1),
                                      op0=ALU.subtract, op1=ALU.mult),
            [rx, rs], [rxn])
        dma("sync", xn_dram.ap()[row:row + pn, :], bap(bn, pn, 2048),
            [rxn], [RD(xn_dram, row, row + pn)])

    ln_tile(0, 3, 0)
    for t in range(16):
        ln_tile(t + 1, 128, 3 + t * 128)
    A.release()


    dl_t = A.take(1024 * 4)
    lndt_t = A.take(1024 * 4)
    dtds_fmb = A.take(2048 * 2)
    ey_fmb = A.take(2048 * 2)
    cd_fm = A.take(16 * 4)
    cdp_fm = A.take(16 * 4)
    cd_bc = A.take(1024 * 4)
    carry = A.take(HID * 4)
    slot12 = A.take(32 * 4)
    lg_sb = A.take(128 * 4)
    bfm = A.take(2048 * 2)
    cfm = A.take(2048 * 2)
    r_dlt, r_lndtt = RA(dl_t, 4096), RA(lndt_t, 4096)
    r_dtdsf = RA(dtds_fmb, 4096, 0, 64)
    r_eyfmb = RA(ey_fmb, 4096, 0, 64)
    r_cdfm, r_cdpfm = RA(cd_fm, 64, 0, 64), RA(cdp_fm, 64, 0, 64)
    r_cdbc = RA(cd_bc, 4096)
    r_carry = RA(carry, 16384)
    r_w12, r_slot12 = RW(0, 32), RA(slot12, 128)
    r_lg = RA(lg_sb, 512)
    r_bfm, r_cfm = RA(bfm, 4096), RA(cfm, 4096)
    dtdump = DS("dtdump", [64, T], F32)

    A.mark()

    xn_fm = A.take(16 * 2064 * 2)

    def xnf(kc, f0, n):
        return bap(xn_fm + kc * 2064 * 2, 128, 2064)[:, f0:f0 + n]

    r_xnfm = RA(xn_fm, 16 * 2064 * 2)
    for kc in range(16):
        dmaT("scalar", bap(xn_fm + kc * 2064 * 2, 128, 2064),
             xn_dram.ap()[0:2064, kc * 128:(kc + 1) * 128],
             [RD(xn_dram, 0, 2064)], [RA(xn_fm + kc * 2064 * 2, 2064 * 2)],
             tag=f"xnfm{kc}")

    A.mark()
    p1_ws = [A.take(2048 * 2) for _ in range(2)]
    p1_cb = [A.take(2052 * 2) for _ in range(2)]
    p1_t = A.take(2048 * 4)
    p1_co = A.take(2048 * 2)

    TOKG = [(0, 512), (512, 512), (1024, 512), (1536, 512), (2048, 3)]

    def conv_tile(ft, xbc_off, rxbc, out_off, rout):
        tb = p1_t
        rt = RA(tb, 8192)
        for i in range(4):
            wcol = fap(c_convw + (ft * 4 + i) * 4, 128, 1)
            src = bap(xbc_off, 128, 2052)[:, i:i + 2048]
            if i == 0:
                vop(lambda e, s=src, w=wcol: e.tensor_scalar(
                    out=fap(tb, 128, 2048), in0=s, scalar1=w, scalar2=None,
                    op0=ALU.mult), [rxbc, r_convw], [rt], tag=f"cv{ft}_{i}")
            else:
                vop(lambda e, s=src, w=wcol: e.scalar_tensor_tensor(
                    out=fap(tb, 128, 2048), in0=s, scalar=w,
                    in1=fap(tb, 128, 2048), op0=ALU.mult, op1=ALU.add),
                    [rxbc, rt, r_convw], [rt], tag=f"cv{ft}_{i}")
        sop(lambda e: e.activation(out=bap(out_off, 128, 2048),
                                   in_=fap(tb, 128, 2048), func=AF.Silu),
            [rt], [rout], tag=f"cvs{ft}")

    for ft in range(34):
        ws = p1_ws[ft % 2]
        rws = RA(ws, 4096)
        dma("gpsimd", bap(ws, 128, 2048),
            bass.AP(winT, 4096 + ft * 128,
                    [[D_IN, 128], [128 * D_IN, 16], [1, 128]]),
            NR, [rws], tag=f"wsl{ft}")
        for gi, (t0, tn) in enumerate(TOKG):
            ps = psA if gi < 4 else psB
            c0 = (t0 if gi < 4 else 0)
            items = []
            for kc in range(16):
                items.append((ps[0:128, c0:c0 + tn],
                              bap(ws, 128, 2048)[:, kc * 128:(kc + 1) * 128],
                              xnf(kc, t0, tn), kc == 0, kc == 15))
            mmg(items, [rws, r_xnfm], [RP(ps, c0, c0 + tn)], tag=f"ip{ft}_{gi}")
        cb = p1_cb[ft % 2]
        rcb = RA(cb, 2052 * 2)
        vop(lambda e, cb=cb: e.tensor_copy(bap(cb, 128, 2052)[:, 0:2048],
                                           psA[0:128, 0:2048]),
            [RP(psA, 0, 2048)], [rcb], tag=f"ev{ft}")
        vop(lambda e, cb=cb: e.tensor_copy(bap(cb, 128, 2052)[:, 2048:2051],
                                           psB[0:128, 0:3]),
            [RP(psB, 0, 3)], [rcb], tag=f"evh{ft}")
        if ft == 32:
            conv_tile(ft, cb, rcb, bfm, r_bfm)
            dma("sync", conv_dump.ap()[ft * 128:(ft + 1) * 128, :],
                bap(bfm, 128, 2048), [r_bfm],
                [RD(conv_dump, ft * 128, (ft + 1) * 128)], tag=f"cd{ft}")
        elif ft == 33:
            conv_tile(ft, cb, rcb, cfm, r_cfm)
        else:
            conv_tile(ft, cb, rcb, p1_co, RA(p1_co, 4096))
            dma("sync", conv_dump.ap()[ft * 128:(ft + 1) * 128, :],
                bap(p1_co, 128, 2048), [RA(p1_co, 4096)],
                [RD(conv_dump, ft * 128, (ft + 1) * 128)], tag=f"cd{ft}")

    ws = p1_ws[0]
    rws = RA(ws, 4096)
    dma("gpsimd", bap(ws, 128, 1024),
        bass.AP(winT, 8448, [[D_IN, 128], [128 * D_IN, 16], [1, 64]]),
        NR, [rws], tag="wsl_dt")
    for g in range(4):
        items = []
        for kc in range(16):
            items.append((psA[0:64, g * 512:(g + 1) * 512],
                          bap(ws, 128, 1024)[:, kc * 64:(kc + 1) * 64],
                          xnf(kc, 3 + g * 512, 512), kc == 0, kc == 15))
        mmg(items, [rws, r_xnfm], [RP(psA, g * 512, (g + 1) * 512, 0, 64)],
            tag=f"dtg{g}")
    vop(lambda e: e.tensor_copy(fap(p1_t, 64, 2048), psA[0:64, 0:2048]),
        [RP(psA, 0, 2048, 0, 64)], [RA(p1_t, 8192, 0, 64)], tag="dtev")
    dma("sync", dtdump.ap(), fap(p1_t, 64, 2048), [RA(p1_t, 8192, 0, 64)],
        [RD(dtdump, 0, 64)], tag="dtst")
    A.release()

    A.mark()
    p2_zs = [A.take(8192 * 2) for _ in range(2)]
    p2_zb = [A.take(512 * 2) for _ in range(2)]

    for tt in range(16):
        items = []
        for kc in range(16):
            items.append((psB[0:128, tt * 8:tt * 8 + 8],
                          xnf(kc, 3 + tt * 128, 128),
                          bap(c_wgate, 128, 128)[:, kc * 8:(kc + 1) * 8],
                          kc == 0, kc == 15))
        mmg(items, [r_xnfm, r_wgate], [RP(psB, tt * 8, tt * 8 + 8)],
            tag=f"lg{tt}")
    vop(lambda e: e.tensor_copy(fap(lg_sb, 128, 128), psB[0:128, 0:128]),
        [RP(psB, 0, 128)], [r_lg], tag="lgev")

    for fg in range(8):
        zs = p2_zs[fg % 2]
        rzs = RA(zs, 16384)
        dma("gpsimd", bap(zs, 128, 8192),
            bass.AP(winT, fg * 512, [[D_IN, 128], [128 * D_IN, 16], [1, 512]]),
            NR, [rzs], tag=f"zw{fg}")
        for tt in range(16):
            c0 = (tt % 4) * 512
            items = []
            for kc in range(16):
                items.append((psA[0:128, c0:c0 + 512],
                              xnf(kc, 3 + tt * 128, 128),
                              bap(zs, 128, 8192)[:, kc * 512:(kc + 1) * 512],
                              kc == 0, kc == 15))
            mmg(items, [rzs, r_xnfm], [RP(psA, c0, c0 + 512)], tag=f"z{fg}_{tt}")
            zb = p2_zb[tt % 2]
            rzb = RA(zb, 1024)
            vop(lambda e, zb=zb, c0=c0: e.tensor_copy(bap(zb, 128, 512),
                                                      psA[0:128, c0:c0 + 512]),
                [RP(psA, c0, c0 + 512)], [rzb], tag=f"zev{fg}_{tt}")
            dma("sync", z_dram.ap()[tt * 128:(tt + 1) * 128, fg * 512:(fg + 1) * 512],
                bap(zb, 128, 512), [rzb],
                [RD(z_dram, tt * 128, (tt + 1) * 128, fg * 512, (fg + 1) * 512)],
                tag=f"zst{fg}_{tt}")
    A.release()
    A.release()
    if DEBUG:
        dma("gpsimd", dbg["logits"].ap(), fap(lg_sb, 128, 128), [r_lg],
            [RD(dbg["logits"], 0, 128)])

    A.mark()
    b_dt = A.take(2048 * 4)
    b_t1 = A.take(2048 * 4)
    b_t2 = A.take(2048 * 4)
    b_dac = A.take(2048 * 4)
    b_lndt = A.take(2048 * 4)
    b_dl = A.take(2048 * 4)
    b_sml = A.take(256 * 4)
    r_bdt = RA(b_dt, 8192, 0, 64)
    r_bt1 = RA(b_t1, 8192, 0, 64)
    r_bt2 = RA(b_t2, 8192, 0, 64)
    r_bdac = RA(b_dac, 8192, 0, 64)
    r_blndt = RA(b_lndt, 8192, 0, 64)
    r_bdl = RA(b_dl, 8192, 0, 64)
    r_bsml = RA(b_sml, 1024, 0, 64)

    dma("sync", fap(b_t1, 64, 2048), dtdump.ap(), [RD(dtdump, 0, 64)], [r_bt1],
        tag="dtld")
    vop(lambda e: e.tensor_scalar(out=fap(b_dt, 64, 2048), in0=fap(b_t1, 64, 2048),
                                  scalar1=fap(c_dbias, 64, 1), scalar2=None,
                                  op0=ALU.add), [r_bt1, r_dbias], [r_bdt], tag="xb")
    sop(lambda e: e.activation(out=fap(b_t2, 64, 2048), in_=fap(b_dt, 64, 2048),
                               func=AF.Abs), [r_bdt], [r_bt2], tag="sp_abs")
    sop(lambda e: e.activation(out=fap(b_t1, 64, 2048), in_=fap(b_t2, 64, 2048),
                               func=AF.Exp, scale=-1.0), [r_bt2], [r_bt1], tag="sp_exp")
    sop(lambda e: e.activation(out=fap(b_t2, 64, 2048), in_=fap(b_t1, 64, 2048),
                               func=AF.Ln, bias=1.0), [r_bt1], [r_bt2], tag="sp_ln")
    vop(lambda e: e.tensor_scalar(out=fap(b_t1, 64, 2048), in0=fap(b_dt, 64, 2048),
                                  scalar1=0.0, scalar2=None, op0=ALU.max),
        [r_bdt], [r_bt1], tag="sp_relu")
    vop(lambda e: e.tensor_add(fap(b_dt, 64, 2048), fap(b_t1, 64, 2048),
                               fap(b_t2, 64, 2048)), [r_bt1, r_bt2], [r_bdt],
        tag="sp_add")
    if DEBUG:
        dma("gpsimd", dbg["dt"].ap(), fap(b_dt, 64, 2048), [r_bdt],
            [RD(dbg["dt"], 0, 64)])
    sop(lambda e: e.activation(out=fap(b_lndt, 64, 2048), in_=fap(b_dt, 64, 2048),
                               func=AF.Ln), [r_bdt], [r_blndt], tag="lndt")
    vop(lambda e: e.tensor_scalar(out=fap(b_t1, 64, 2048), in0=fap(b_dt, 64, 2048),
                                  scalar1=fap(c_nA, 64, 1), scalar2=None,
                                  op0=ALU.mult), [r_bdt, r_nA], [r_bt1], tag="dA")
    for c in range(NCHUNK):
        vop(lambda e, c=c: e.tensor_tensor_scan(
            out=fap(b_dac + c * 512, 64, 128), data0=fap(b_t1 + c * 512, 64, 128),
            data1=fap(c_zero, 64, 128), initial=0.0, op0=ALU.add, op1=ALU.add),
            [r_bt1, r_zero], [r_bdac], tag=f"dac{c}")
    if DEBUG:
        dma("gpsimd", dbg["dac"].ap(), fap(b_dac, 64, 2048), [r_bdac],
            [RD(dbg["dac"], 0, 64)])
    vop(lambda e: e.tensor_sub(fap(b_dl, 64, 2048), fap(b_dac, 64, 2048),
                               fap(b_lndt, 64, 2048)), [r_bdac, r_blndt], [r_bdl],
        tag="dl")
    sop(lambda e: e.activation(out=bap(ey_fmb, 64, 2048), in_=fap(b_dac, 64, 2048),
                               func=AF.Exp), [r_bdac], [r_eyfmb], tag="ey")
    vop(lambda e: e.tensor_scalar(out=bap(b_t2, 64, HID), in0=bap(c_rsel, 64, HID),
                                  scalar1=fap(c_dpart, 64, 1), scalar2=None,
                                  op0=ALU.mult), [r_rsel, r_dpart], [r_bt2],
        tag="rselD")
    for g in range(8):
        mmg([(psB[0:128, 1024 + (g % 2) * 512:1536 + (g % 2) * 512],
              bap(c_ones64, 64, 128), bap(b_t2, 64, HID)[:, g * 512:(g + 1) * 512],
              True, True)], [r_ones64, r_bt2],
            [RP(psB, 1024 + (g % 2) * 512, 1536 + (g % 2) * 512)], tag=f"dxp{g}")
        sop(lambda e, g=g: e.copy(bap(d_exp + g * 1024, 128, 512),
                                  psB[0:128, 1024 + (g % 2) * 512:1536 + (g % 2) * 512]),
            [RP(psB, 1024 + (g % 2) * 512, 1536 + (g % 2) * 512)],
            [RA(d_exp + g * 1024, 1024)], tag=f"dxc{g}")
    for c in range(NCHUNK):
        sop(lambda e, c=c: e.activation(out=fap(cd_fm + c * 4, 64, 1),
                                        in_=fap(b_dac + c * 512 + 508, 64, 1),
                                        func=AF.Exp), [r_bdac], [r_cdfm],
            tag=f"cd{c}")
    vop(lambda e: e.memset(fap(cdp_fm, 64, 1), 1.0), NR, [r_cdpfm], tag="cdp0")
    vop(lambda e: e.tensor_tensor_scan(
        out=fap(cdp_fm + 4, 64, 15), data0=fap(cd_fm, 64, 15),
        data1=fap(c_zero, 64, 15), initial=1.0, op0=ALU.mult, op1=ALU.add),
        [r_cdfm, r_zero], [r_cdpfm], tag="cdp")
    for c in range(NCHUNK):
        dma("sync", bass.AP(rowdump, c * 8192, [[128, 64], [1, 128]]),
            fap(b_dac + c * 512, 64, 128), [r_bdac], [RD(rowdump, c, c + 1)],
            tag=f"rd{c}")
    dma("sync", bass.AP(smalldump, 0, [[16, 64], [1, 16]]), fap(cd_fm, 64, 16),
        [r_cdfm], [RD(smalldump, 0, 1024)], tag="cddump")
    dma("sync", fap(cd_bc, 128, 1024), bass.AP(smalldump, 0, [[0, 128], [1, 1024]]),
        [RD(smalldump, 0, 1024)], [r_cdbc], tag="cdbc")
    for c in range(NCHUNK):
        vop(lambda e, c=c: e.tensor_scalar(
            out=fap(b_sml, 64, 128), in0=fap(b_dac + c * 512, 64, 128),
            scalar1=fap(b_dac + c * 512 + 127 * 4, 64, 1), scalar2=None,
            op0=ALU.subtract), [r_bdac], [r_bsml], tag=f"ds{c}")
        sop(lambda e: e.activation(out=fap(b_sml, 64, 128), in_=fap(b_sml, 64, 128),
                                   func=AF.Exp, scale=-1.0), [r_bsml], [r_bsml],
            tag=f"dse{c}")
        vop(lambda e, c=c: e.tensor_mul(bap(dtds_fmb + c * 256, 64, 128),
                                        fap(b_dt + c * 512, 64, 128),
                                        fap(b_sml, 64, 128)),
            [r_bdt, r_bsml], [r_dtdsf], tag=f"dtds{c}")
        for (src, dst, rdst) in ((b_dl + c * 512, dl_t + c * 256, r_dlt),
                                 (b_lndt + c * 512, lndt_t + c * 256, r_lndtt)):
            kb.op("tensor",
                  lambda e, src=src: e.transpose(psB[0:128, 1920:1984],
                                                 fap(src, 64, 128),
                                                 fap(c_ident, 64, 64)),
                  reads=[RA(src, 512, 0, 64), r_ident],
                  writes=[RP(psB, 1920, 1984)], tag=f"tr{c}")
            vop(lambda e, dst=dst: e.tensor_copy(fap(dst, 128, 64),
                                                 psB[0:128, 1920:1984]),
                [RP(psB, 1920, 1984)], [rdst], tag=f"trc{c}")
    if DEBUG:
        dma("gpsimd", dbg["dlt"].ap(), fap(dl_t, 128, 1024), [r_dlt],
            [RD(dbg["dlt"], 0, 128)])
        dma("gpsimd", dbg["lndtt"].ap(), fap(lndt_t, 128, 1024), [r_lndtt],
            [RD(dbg["lndtt"], 0, 128)])

        dma("gpsimd", dbg["eyfmb"].ap(), bap(ey_fmb, 64, 2048), [r_eyfmb],
            [RD(dbg["eyfmb"], 0, 64)])
        dma("gpsimd", dbg["cdbc"].ap(), fap(cd_bc, 128, 1024), [r_cdbc],
            [RD(dbg["cdbc"], 0, 128)])
    A.release()

    A.mark()
    p3_mask1 = A.take(8 * 4)
    p3_mask2 = A.take(8 * 4)
    p3_lg2 = A.take(8 * 4)
    p3_m12 = A.take(8 * 4)
    p3_em = A.take(2048 * 4)
    p3_pos = A.take(2048 * 4)
    p3_g0 = A.take(2048 * 4)
    p3_ov = A.take(2048 * 4)
    p3_gsc = A.take(2048 * 4)
    p3_eb = A.take(16 * 4)
    p3_tm = A.take(64 * 4)
    p3_i = A.take(64 * 4)
    r_m = RW(32, 96)
    r_mask1, r_mask2 = RA(p3_mask1, 32), RA(p3_mask2, 32)
    r_lg2, r_m12 = RA(p3_lg2, 32), RA(p3_m12, 32)
    r_em = RA(p3_em, 8192, 0, 8)
    r_pos = RA(p3_pos, 8192, 0, 8)
    r_g0 = RA(p3_g0, 8192, 0, 8)
    r_ov = RA(p3_ov, 8192, 0, 8)
    r_gsc = RA(p3_gsc, 8192, 0, 8)
    r_eb = RA(p3_eb, 64, 0, 8)
    r_tm = RA(p3_tm, 256)
    r_i3 = RA(p3_i, 256)

    gop(lambda e: e.memset(ix_h[:, 0:64], 3), NR, [RX(0, 64)], tag="pf0")
    dma("gpsimd", bass.AP(idx_dram, 0, [[64, 128], [1, 64]]),
        ix_h[:, 0:64], [RX(0, 64)], [RD(idx_dram, 0, 8192)], tag="pf1")

    for tt in range(16):
        lg = fap(lg_sb, 128, 128)[:, tt * 8:(tt + 1) * 8]
        mb = 32 + tt * 4
        vop(lambda e, lg=lg, mb=mb: e.reduce_max(wap(mb, 1), lg, axis=AX.X),
            [r_lg], [r_m], tag=f"r_m1_{tt}")
        vop(lambda e, lg=lg, mb=mb: e.tensor_scalar(
            out=fap(p3_mask1, 128, 8), in0=lg, scalar1=wap(mb, 1),
            scalar2=None, op0=ALU.is_equal), [r_lg, r_m], [r_mask1], tag=f"r_k1_{tt}")
        vop(lambda e, lg=lg: e.scalar_tensor_tensor(
            out=fap(p3_lg2, 128, 8), in0=fap(p3_mask1, 128, 8), scalar=-1e30,
            in1=lg, op0=ALU.mult, op1=ALU.add), [r_lg, r_mask1], [r_lg2],
            tag=f"r_l2_{tt}")
        vop(lambda e, mb=mb: e.reduce_max(wap(mb + 1, 1), fap(p3_lg2, 128, 8),
                                          axis=AX.X), [r_lg2], [r_m], tag=f"r_m2_{tt}")
        vop(lambda e, mb=mb: e.tensor_scalar(
            out=fap(p3_mask2, 128, 8), in0=fap(p3_lg2, 128, 8),
            scalar1=wap(mb + 1, 1), scalar2=None, op0=ALU.is_equal),
            [r_lg2, r_m], [r_mask2], tag=f"r_k2_{tt}")
        vop(lambda e, mb=mb: e.tensor_sub(wap(mb + 2, 1), wap(mb + 1, 1),
                                          wap(mb, 1)), [r_m], [r_m],
            tag=f"r_dx_{tt}")
        sop(lambda e, mb=mb: e.activation(out=wap(mb + 2, 1),
                                          in_=wap(mb + 2, 1), func=AF.Exp),
            [r_m], [r_m], tag=f"r_ed_{tt}")
        vop(lambda e, mb=mb: e.tensor_scalar(out=wap(mb + 3, 1),
                                             in0=wap(mb + 2, 1), scalar1=1.0,
                                             scalar2=None, op0=ALU.add),
            [r_m], [r_m], tag=f"r_od_{tt}")
        vop(lambda e, mb=mb, tt=tt: e.reciprocal(wap(tt * 2, 1),
                                          wap(mb + 3, 1)), [r_m], [r_w12],
            tag=f"r_w1_{tt}")
        vop(lambda e, mb=mb, tt=tt: e.tensor_mul(wap(tt * 2 + 1, 1),
                                                 wap(mb + 2, 1),
                                                 wap(tt * 2, 1)),
            [r_m, r_w12], [r_w12], tag=f"r_w2_{tt}")
        vop(lambda e: e.tensor_add(fap(p3_m12, 128, 8), fap(p3_mask1, 128, 8),
                                   fap(p3_mask2, 128, 8)), [r_mask1, r_mask2],
            [r_m12], tag=f"r_s_{tt}")
        kb.op("tensor", lambda e: e.transpose(psB[0:8, 1664:1792],
                                              fap(p3_m12, 128, 8),
                                              fap(c_ident, 128, 128)),
              reads=[r_m12, r_ident], writes=[RP(psB, 1664, 1792, 0, 8)],
              tag=f"r_t_{tt}")
        vop(lambda e, tt=tt: e.tensor_copy(fap(p3_em + tt * 512, 8, 128),
                                           psB[0:8, 1664:1792]),
            [RP(psB, 1664, 1792, 0, 8)], [r_em], tag=f"r_e_{tt}")

    vop(lambda e: e.memset(fap(p3_g0, 8, 2048), 0.0), NR, [r_g0], tag="pz")
    vop(lambda e: e.tensor_tensor_scan(out=fap(p3_pos, 8, 2048),
                                       data0=fap(p3_em, 8, 2048),
                                       data1=fap(p3_g0, 8, 2048),
                                       initial=0.0, op0=ALU.add, op1=ALU.add),
        [r_em, r_g0], [r_pos], tag="pos")
    gop(lambda e: e.iota(iap(p3_eb + 32, 8, 1), [[1, 1]], channel_multiplier=1024),
        NR, [r_eb], tag="eb_i")
    vop(lambda e: e.tensor_copy(fap(p3_eb, 8, 1), iap(p3_eb + 32, 8, 1)),
        [r_eb], [r_eb], tag="eb_f")
    vop(lambda e: e.tensor_scalar(out=fap(p3_eb + 4, 8, 1), in0=fap(p3_eb, 8, 1),
                                  scalar1=1023.0, scalar2=None, op0=ALU.add),
        [r_eb], [r_eb], tag="eb1023")
    vop(lambda e: e.tensor_scalar(out=fap(p3_g0, 8, 2048), in0=fap(p3_pos, 8, 2048),
                                  scalar1=fap(p3_eb, 8, 1), scalar2=1.0,
                                  op0=ALU.add, op1=ALU.subtract),
        [r_pos, r_eb], [r_g0], tag="g0")
    vop(lambda e: e.tensor_scalar(out=fap(p3_ov, 8, 2048), in0=fap(p3_pos, 8, 2048),
                                  scalar1=1024.5, scalar2=None, op0=ALU.is_ge),
        [r_pos], [r_ov], tag="ovv")
    vop(lambda e: e.tensor_scalar(out=fap(p3_gsc, 8, 2048),
                                  in0=fap(p3_em, 8, 2048), scalar1=-1.0,
                                  scalar2=1.0, op0=ALU.mult, op1=ALU.add),
        [r_em], [r_gsc], tag="gsc0")
    vop(lambda e: e.tensor_add(fap(p3_gsc, 8, 2048), fap(p3_gsc, 8, 2048),
                               fap(p3_ov, 8, 2048)), [r_gsc, r_ov], [r_gsc],
        tag="gsc1")
    vop(lambda e: e.scalar_tensor_tensor(out=fap(p3_gsc, 8, 2048),
                                         in0=fap(p3_gsc, 8, 2048), scalar=1e9,
                                         in1=fap(p3_g0, 8, 2048), op0=ALU.mult,
                                         op1=ALU.add), [r_gsc, r_g0], [r_gsc],
        tag="gsc")
    vop(lambda e: e.tensor_scalar(out=fap(p3_pos, 8, 2048), in0=fap(p3_g0, 8, 2048),
                                  scalar1=fap(p3_eb + 4, 8, 1), scalar2=None,
                                  op0=ALU.min), [r_g0, r_eb], [r_pos], tag="geff")

    for tt in range(16):
        lg = fap(lg_sb, 128, 128)[:, tt * 8:(tt + 1) * 8]
        mb = 32 + tt * 4
        vop(lambda e, lg=lg, mb=mb: e.tensor_scalar(
            out=fap(p3_mask1, 128, 8), in0=lg, scalar1=wap(mb, 1),
            scalar2=None, op0=ALU.is_equal), [r_lg, r_m], [r_mask1], tag=f"s_k1_{tt}")
        vop(lambda e, lg=lg, mb=mb: e.scalar_tensor_tensor(
            out=fap(p3_lg2, 128, 8), in0=fap(p3_mask1, 128, 8), scalar=-1e30,
            in1=lg, op0=ALU.mult, op1=ALU.add), [r_lg, r_mask1], [r_lg2],
            tag=f"s_l2_{tt}")
        vop(lambda e, mb=mb: e.tensor_scalar(
            out=fap(p3_mask2, 128, 8), in0=fap(p3_lg2, 128, 8),
            scalar1=wap(mb + 1, 1), scalar2=None, op0=ALU.is_equal),
            [r_lg2, r_m], [r_mask2], tag=f"s_k2_{tt}")
        kb.op("tensor", lambda e, tt=tt: e.transpose(
            psB[0:128, 1792:1800], fap(p3_gsc + tt * 512, 8, 128),
            fap(c_ident, 8, 8)), reads=[r_gsc, r_ident],
            writes=[RP(psB, 1792, 1800)], tag=f"s_tg_{tt}")
        kb.op("tensor", lambda e, tt=tt: e.transpose(
            psB[0:128, 1800:1808], fap(p3_pos + tt * 512, 8, 128),
            fap(c_ident, 8, 8)), reads=[r_pos, r_ident],
            writes=[RP(psB, 1800, 1808)], tag=f"s_te_{tt}")
        vop(lambda e: e.tensor_copy(fap(p3_tm, 128, 8), psB[0:128, 1792:1800]),
            [RP(psB, 1792, 1800)], [r_tm], tag=f"s_cg_{tt}")
        vop(lambda e: e.tensor_copy(fap(p3_tm + 32, 128, 8), psB[0:128, 1800:1808]),
            [RP(psB, 1800, 1808)], [r_tm], tag=f"s_ce_{tt}")
        vop(lambda e: e.tensor_copy(ix_h[:, 0:8], psB[0:128, 1792:1800]),
            [RP(psB, 1792, 1800)], [RX(0, 8)], tag=f"s_ci_{tt}")
        gop(lambda e, tt=tt: e.iota(ix_h[:, 8:9], [[1, 1]],
                                    base=3 + tt * 128, channel_multiplier=1),
            NR, [RX(8, 9)], tag=f"s_io_{tt}")
        for e_ in range(8):
            kb.op("gpsimd", lambda e, e_=e_: e.indirect_dma_start(
                out=idx_dram.ap(),
                out_offset=IOA(ix_h[:, e_:e_ + 1], 0),
                in_=ix_h[:, 8:9], in_offset=None,
                bounds_check=bnd_reg, oob_is_err=False),
                reads=[RX(0, 9)], writes=[RD(idx_dram, 0, 8192)], dma=True,
                tag=f"s_sc_{tt}_{e_}")
        vop(lambda e: e.tensor_sub(fap(p3_lg2, 128, 8),
                                   fap(p3_tm, 128, 8), fap(p3_tm + 32, 128, 8)),
            [r_tm], [r_lg2], tag=f"s_ov_{tt}")
        vop(lambda e: e.tensor_scalar(out=fap(p3_lg2, 128, 8),
                                      in0=fap(p3_lg2, 128, 8), scalar1=0.5,
                                      scalar2=None, op0=ALU.is_ge),
            [r_lg2], [r_lg2], tag=f"s_ov2_{tt}")
        for k, (rmask, moff) in enumerate(((r_mask1, p3_mask1), (r_mask2, p3_mask2))):
            vop(lambda e, moff=moff: e.tensor_mul(fap(p3_m12, 128, 8),
                                                  fap(moff, 128, 8),
                                                  fap(p3_tm + 32, 128, 8)),
                [rmask, r_tm], [r_m12], tag=f"s_sm_{tt}_{k}")
            vop(lambda e, mb=mb: e.reduce_sum(
                wap(mb + 3, 1),
                fap(p3_m12, 128, 8), axis=AX.X), [r_m12], [r_m], tag=f"s_sr_{tt}_{k}")
            vop(lambda e, tt=tt, k=k, mb=mb: e.tensor_copy(
                ix_h[:, 16 + tt * 2 + k:16 + tt * 2 + k + 1],
                wap(mb + 3, 1)), [r_m],
                [RX(16 + tt * 2 + k, 16 + tt * 2 + k + 1)],
                tag=f"s_sc2_{tt}_{k}")
            vop(lambda e, moff=moff: e.tensor_mul(fap(p3_m12, 128, 8),
                                                  fap(moff, 128, 8),
                                                  fap(p3_lg2, 128, 8)),
                [rmask, r_lg2], [r_m12], tag=f"s_wm_{tt}_{k}")
            vop(lambda e, mb=mb: e.reduce_sum(wap(mb + 3, 1),
                                              fap(p3_m12, 128, 8), axis=AX.X),
                [r_m12], [r_m], tag=f"s_wr_{tt}_{k}")
            vop(lambda e, mb=mb: e.tensor_scalar(
                out=wap(mb + 3, 1),
                in0=wap(mb + 3, 1), scalar1=-1.0, scalar2=1.0,
                op0=ALU.mult, op1=ALU.add), [r_m], [r_m], tag=f"s_wn_{tt}_{k}")
            vop(lambda e, tt=tt, k=k, mb=mb: e.tensor_mul(
                wap(tt * 2 + k, 1),
                wap(tt * 2 + k, 1),
                wap(mb + 3, 1)), [r_m, r_w12], [r_w12],
                tag=f"s_wf_{tt}_{k}")
    if DEBUG:
        dma("gpsimd", dbg["w12"].ap(), wap(0, 32), [r_w12],
            [RD(dbg["w12"], 0, 128)])
        dma("gpsimd", dbg["slots"].ap(), ix_h[:, 16:48], [RX(16, 48)],
            [RD(dbg["slots"], 0, 128)])
    A.release()


    sinb = A.take(HID * 2)
    r_sinb = RA(sinb, HID * 2)
    A.mark()
    p4_xhb = [A.take(4224 * 2) for _ in range(2)]
    p4_bc = A.take(8192 * 4)
    p4_gt = A.take(128 * 2)
    p4_gt4 = A.take(512 * 2)
    p4_cdeb = A.take(128 * 2)
    p4_cdx = A.take(512 * 4)
    p4_dtx = A.take(512 * 2)
    p4_dterm = A.take(512 * 4)
    p4_t1m = [A.take(512 * 4) for _ in range(2)]
    p4_eo = [A.take(512 * 2) for _ in range(2)]
    p4_mt = [A.take(1024 * 2) for _ in range(2)]
    p4_cb = A.take(4096 * 2)
    p4_eyx = [A.take(512 * 4) for _ in range(2)]
    p4_t1f = A.take(512 * 4)
    p4_xd = A.take(4096 * 2)
    p4_y = [A.take(512 * 4) for _ in range(2)]
    r_bc = RA(p4_bc, 32768)
    r_gt = RA(p4_gt, 256)
    r_gt4 = RA(p4_gt4, 1024)
    r_cdeb = RA(p4_cdeb, 256, 0, 64)
    r_cdx = RA(p4_cdx, 2048)
    r_dtx = RA(p4_dtx, 1024)
    r_dterm = RA(p4_dterm, 2048)
    r_cb = RA(p4_cb, 8192)
    r_t1f = RA(p4_t1f, 2048)
    r_xd = RA(p4_xd, 8192)

    vop(lambda e: e.memset(fap(carry, 128, 4096), 0.0), NR, [r_carry], tag="cz")

    for c in range(NCHUNK):
        xq = p4_xhb[c % 2]
        r_xq = RA(xq, 8448)
        dmaT("scalar", bap(xq, 128, 4224),
             conv_dump.ap()[0:4224, c * 128:(c + 1) * 128],
             [RD(conv_dump, 0, 4224)], [r_xq], tag=f"c{c}_xh")
        dma("sync", fap(p4_bc, 128, 8192),
            bass.AP(rowdump, c * 8192, [[0, 128], [1, 8192]]),
            [RD(rowdump, c, c + 1)], [r_bc], tag=f"c{c}_bc")
        mmg([(psB[0:128, 0:128], bap(bfm, 128, 2048)[:, c * 128:(c + 1) * 128],
              bap(cfm, 128, 2048)[:, c * 128:(c + 1) * 128], True, True)],
            [r_bfm, r_cfm], [RP(psB, 0, 128)], tag=f"c{c}_gt")
        vop(lambda e: e.tensor_mul(bap(p4_gt, 128, 128), psB[0:128, 0:128],
                                   fap(c_trib, 128, 128)),
            [RP(psB, 0, 128), r_trib], [r_gt], tag=f"c{c}_gtm")
        for rr in range(4):
            vop(lambda e, rr=rr: e.tensor_copy(
                bap(p4_gt4, 128, 512)[:, rr * 128:(rr + 1) * 128],
                bap(p4_gt, 128, 128)), [r_gt], [r_gt4], tag=f"c{c}_gt4_{rr}")
        vop(lambda e, c=c: e.tensor_scalar(out=bap(p4_cdeb, 64, 128),
                                           in0=bap(c_ones64, 64, 128),
                                           scalar1=fap(cd_fm + c * 4, 64, 1),
                                           scalar2=None, op0=ALU.mult),
            [r_ones64, r_cdfm], [r_cdeb], tag=f"c{c}_cdb")
        vop(lambda e: e.tensor_copy(bap(p4_cb, 128, 4096), fap(carry, 128, 4096)),
            [r_carry], [r_cb], tag=f"c{c}_cb")
        def build_mt_quad(q):
            t1 = p4_t1m[q % 2]
            eo = p4_eo[q % 2]
            r_t1, r_eo = RA(t1, 2048), RA(eo, 1024)
            for j in range(4):
                h = q * 4 + j
                vop(lambda e, t1=t1, j=j, h=h, c=c: e.tensor_scalar(
                    out=fap(t1 + j * 512, 128, 128),
                    in0=fap(p4_bc + h * 512, 128, 128),
                    scalar1=fap(dl_t + (c * 64 + h) * 4, 128, 1),
                    scalar2=fap(lndt_t + (c * 64 + h) * 4, 128, 1),
                    op0=ALU.subtract, op1=ALU.min),
                    [r_bc, r_dlt, r_lndtt], [r_t1], tag=f"c{c}_t1_{h}")
            sop(lambda e, t1=t1, eo=eo: e.activation(out=bap(eo, 128, 512),
                                                     in_=fap(t1, 128, 512),
                                                     func=AF.Exp),
                [r_t1], [r_eo], tag=f"c{c}_ex{q}")
            g = q // 2
            mtb = p4_mt[g % 2]
            r_mtb = RA(mtb, 2048)
            vop(lambda e, eo=eo, mtb=mtb, q=q: e.tensor_mul(
                bap(mtb, 128, 1024)[:, (q % 2) * 512:(q % 2) * 512 + 512],
                bap(eo, 128, 512), bap(p4_gt4, 128, 512)),
                [r_eo, r_gt4], [r_mtb], tag=f"c{c}_mtm_{q}")

        for g in range(8):
            build_mt_quad(2 * g)
            build_mt_quad(2 * g + 1)
            mtb = p4_mt[g % 2]
            r_mtb = RA(mtb, 2048)
            cd0 = (g % 2) * 512
            co0 = 1024 + (g % 2) * 512
            if c > 0:
                mmg([(psA[0:128, co0:co0 + 512],
                      bap(cfm, 128, 2048)[:, c * 128:(c + 1) * 128],
                      bap(p4_cb, 128, 4096)[:, g * 512:(g + 1) * 512], True, True)],
                    [r_cfm, r_cb], [RP(psA, co0, co0 + 512)], tag=f"c{c}_yo{g}")
            items = []
            for hh in range(8):
                h = g * 8 + hh
                items.append((psA[0:128, cd0 + hh * 64:cd0 + hh * 64 + 64],
                              bap(mtb, 128, 1024)[:, hh * 128:(hh + 1) * 128],
                              bap(xq, 128, 4224)[:, h * 64:(h + 1) * 64],
                              True, True))
            mmg(items, [r_mtb, r_xq], [RP(psA, cd0, cd0 + 512)], tag=f"c{c}_yd{g}")
            ex0 = 512 + (g % 2) * 512
            mmg([(psB[0:128, ex0:ex0 + 512],
                  bap(ey_fmb, 64, 2048)[:, c * 128:(c + 1) * 128],
                  bap(c_rsel, 64, HID)[:, g * 512:(g + 1) * 512], True, True)],
                [r_eyfmb, r_rsel], [RP(psB, ex0, ex0 + 512)], tag=f"c{c}_ey{g}")
            eyx = p4_eyx[g % 2]
            r_eyx = RA(eyx, 2048)
            sop(lambda e, eyx=eyx, ex0=ex0: e.copy(
                fap(eyx, 128, 512), psB[0:128, ex0:ex0 + 512]),
                [RP(psB, ex0, ex0 + 512)], [r_eyx], tag=f"c{c}_eyc{g}")
            vop(lambda e, g=g, xq=xq: e.tensor_mul(
                fap(p4_dterm, 128, 512), bap(xq, 128, 4224)[:, g * 512:(g + 1) * 512],
                bap(d_exp + g * 1024, 128, 512)),
                [r_xq, r_dexp], [r_dterm], tag=f"c{c}_yD{g}")
            if c > 0:
                vop(lambda e, eyx=eyx, co0=co0: e.tensor_mul(
                    fap(p4_t1f, 128, 512), psA[0:128, co0:co0 + 512],
                    fap(eyx, 128, 512)),
                    [RP(psA, co0, co0 + 512), r_eyx], [r_t1f], tag=f"c{c}_ym{g}")
                vop(lambda e: e.tensor_add(fap(p4_t1f, 128, 512),
                                           fap(p4_t1f, 128, 512),
                                           fap(p4_dterm, 128, 512)),
                    [r_t1f, r_dterm], [r_t1f], tag=f"c{c}_yd2{g}")
            else:
                vop(lambda e: e.tensor_copy(fap(p4_t1f, 128, 512),
                                            fap(p4_dterm, 128, 512)),
                    [r_dterm], [r_t1f], tag=f"c{c}_yd2{g}")
            yb = p4_y[g % 2]
            r_yb = RA(yb, 2048)
            vop(lambda e, cd0=cd0, yb=yb: e.tensor_add(
                fap(yb, 128, 512), fap(p4_t1f, 128, 512),
                psA[0:128, cd0:cd0 + 512]),
                [r_t1f, RP(psA, cd0, cd0 + 512)], [r_yb], tag=f"c{c}_ya{g}")
            dma("sync", ylocal_dram.ap()[c * 128:(c + 1) * 128,
                                         g * 512:(g + 1) * 512],
                fap(yb, 128, 512), [r_yb],
                [RD(ylocal_dram, c * 128, (c + 1) * 128, g * 512, (g + 1) * 512)],
                tag=f"c{c}_yst{g}")
        for g in range(8):
            mmg([(psB[0:128, 1536:1792],
                  bap(dtds_fmb, 64, 2048)[:, c * 128:(c + 1) * 128],
                  bap(c_rsel, 64, HID)[:, g * 512:g * 512 + 256], True, True),
                 (psB[0:128, 1792:2048],
                  bap(dtds_fmb, 64, 2048)[:, c * 128:(c + 1) * 128],
                  bap(c_rsel, 64, HID)[:, g * 512 + 256:(g + 1) * 512], True, True)],
                [r_dtdsf, r_rsel], [RP(psB, 1536, 2048)], tag=f"c{c}_dxm{g}")
            sop(lambda e: e.copy(bap(p4_dtx, 128, 512), psB[0:128, 1536:2048]),
                [RP(psB, 1536, 2048)], [r_dtx], tag=f"c{c}_dxc{g}")
            vop(lambda e, g=g, xq=xq: e.tensor_mul(
                bap(p4_xd, 128, 4096)[:, g * 512:(g + 1) * 512],
                bap(xq, 128, 4224)[:, g * 512:(g + 1) * 512],
                bap(p4_dtx, 128, 512)), [r_xq, r_dtx], [r_xd], tag=f"c{c}_xd{g}")
        for g in range(8):
            c0 = (g % 4) * 512
            mmg([(psA[0:128, c0:c0 + 512], bap(xq, 128, 4224)[:, 4096:4224],
                  bap(p4_xd, 128, 4096)[:, g * 512:(g + 1) * 512], True, True)],
                [r_xq, r_xd], [RP(psA, c0, c0 + 512)], tag=f"c{c}_st{g}")
            e0x = 512 + (g % 2) * 512
            mmg([(psB[0:128, e0x:e0x + 512], bap(p4_cdeb, 64, 128),
                  bap(c_rsel, 64, HID)[:, g * 512:(g + 1) * 512], True, True)],
                [r_cdeb, r_rsel], [RP(psB, e0x, e0x + 512)], tag=f"c{c}_cxm{g}")
            sop(lambda e, e0x=e0x: e.copy(fap(p4_cdx, 128, 512),
                                          psB[0:128, e0x:e0x + 512]),
                [RP(psB, e0x, e0x + 512)], [r_cdx], tag=f"c{c}_cxc{g}")
            vop(lambda e, g=g: e.tensor_mul(
                fap(carry + g * 2048, 128, 512), fap(carry + g * 2048, 128, 512),
                fap(p4_cdx, 128, 512)),
                [RA(carry + g * 2048, 2048), r_cdx], [RA(carry + g * 2048, 2048)],
                tag=f"c{c}_cm{g}")
            vop(lambda e, g=g, c0=c0: e.tensor_add(
                fap(carry + g * 2048, 128, 512), fap(carry + g * 2048, 128, 512),
                psA[0:128, c0:c0 + 512]),
                [RA(carry + g * 2048, 2048), RP(psA, c0, c0 + 512)],
                [RA(carry + g * 2048, 2048)], tag=f"c{c}_ca{g}")
    if DEBUG:
        dma("gpsimd", dbg["carry"].ap(), fap(carry, 128, 4096), [r_carry],
            [RD(dbg["carry"], 0, 128)])

    vop(lambda e: e.tensor_scalar(out=fap(carry, 128, 4096),
                                  in0=fap(carry, 128, 4096),
                                  scalar1=fap(c_gates, 128, 1), scalar2=None,
                                  op0=ALU.mult), [r_carry, r_gates], [r_carry],
        tag="smask")
    dma("sync", cc_in.ap(), fap(carry, 128, 4096), [r_carry],
        [RD(cc_in, 0, 128)], tag="ccw")
    kb.op("gpsimd", lambda e: e.collective_compute(
        "AllReduce", ALU.add, replica_groups=[[0, 1], [2, 3], [4, 5], [6, 7]],
        ins=[cc_in.ap().opt()], outs=[cc_out.ap().opt()]),
        reads=[RD(cc_in, 0, 128)], writes=[RD(cc_out, 0, 128)], inc=("cc", 1),
        tag="ccall")
    dma("sync", fap(carry, 128, 4096), cc_out.ap(), [RD(cc_out, 0, 128)],
        [r_carry], tag="ccr")
    vop(lambda e: e.tensor_scalar(out=bap(sinb, 128, 4096),
                                  in0=fap(carry, 128, 4096),
                                  scalar1=fap(c_gates + 4, 128, 1), scalar2=None,
                                  op0=ALU.mult), [r_carry, r_gates], [r_sinb],
        tag="srecv")
    A.release()

    A.mark()
    p6_u = A.take(16384 * 2)
    p6_wo = A.take(16384)
    p6_xf = A.take(8192)
    p6_it = A.take(4 * 2048 * 2)
    p6_s0 = A.take(2048 * 4)
    p6_gx = [A.take(2048 * 2)] * 2
    p6_oe = A.take(2048 * 2)
    r_u0, r_u1 = RA(p6_u, 16384), RA(p6_u + 16384, 16384)
    r_wo = RA(p6_wo, 16384)
    r_xf = RA(p6_xf, 8192)
    r_it = RA(p6_it, 16384)
    r_s0 = RA(p6_s0, 8192)

    for ex in range(NEXP):
        for g in range(8):
            dma("sync", ix_h[:, 48 + g:49 + g],
                bass.AP(idx_dram, ex * CAP + g * 128, [[1, 128], [1, 1]]),
                [RD(idx_dram, ex * CAP + g * 128, ex * CAP + g * 128 + 128)],
                [RX(48 + g, 49 + g)], tag=f"f{ex}_ix{g}")
        for g in range(8):
            gx = p6_gx[g % 2]
            r_gx = RA(gx, 4096)
            kb.op("gpsimd", lambda e, gx=gx, g=g: e.indirect_dma_start(
                out=bap(gx, 128, 2048), out_offset=None,
                in_=xn_dram.ap(), in_offset=IOA(ix_h[:, 48 + g:49 + g], 0)),
                reads=[RX(48 + g, 49 + g), RD(xn_dram, 0, 2064)], writes=[r_gx],
                dma=True, tag=f"f{ex}_g{g}")
            dma("gpsimd", xsorted.ap()[ex * CAP + g * 128:ex * CAP + (g + 1) * 128, :],
                bap(gx, 128, 2048), [r_gx],
                [RD(xsorted, ex * CAP + g * 128, ex * CAP + (g + 1) * 128)],
                tag=f"f{ex}_xs{g}")
        for mh in range(2):
            dma("gpsimd", bap(p6_u, 128, 8192),
                bass.AP(wi0, ex * DIM * MM + mh * 512,
                        [[MM, 128], [128 * MM, 16], [1, 512]]), NR, [r_u0],
                tag=f"f{ex}_w0{mh}")
            dma("gpsimd", bap(p6_u + 16384, 128, 8192),
                bass.AP(wi1, ex * DIM * MM + mh * 512,
                        [[MM, 128], [128 * MM, 16], [1, 512]]), NR, [r_u1],
                tag=f"f{ex}_w1{mh}")
            for cg in range(2):
                for kh in range(2):
                    for k8 in range(8):
                        kc = kh * 8 + k8
                        dmaT("scalar", bap(p6_xf + k8 * 1024, 128, 512),
                             xsorted.ap()[ex * CAP + cg * 512:ex * CAP + (cg + 1) * 512,
                                          kc * 128:(kc + 1) * 128],
                             [RD(xsorted, ex * CAP + cg * 512, ex * CAP + (cg + 1) * 512)],
                             [RA(p6_xf + k8 * 1024, 1024)], tag=f"f{ex}_xf{mh}_{cg}_{kc}")
                    for k8 in range(8):
                        kc = kh * 8 + k8
                        items = []
                        for mt in range(4):
                            items.append((psA[0:128, mt * 512:(mt + 1) * 512],
                                          bap(p6_u, 128, 8192)[:, kc * 512 + mt * 128:
                                                               kc * 512 + (mt + 1) * 128],
                                          bap(p6_xf + k8 * 1024, 128, 512),
                                          kc == 0, kc == 15))
                        for mt in range(4):
                            items.append((psB[0:128, mt * 512:(mt + 1) * 512],
                                          bap(p6_u + 16384, 128, 8192)[:, kc * 512 + mt * 128:
                                                                       kc * 512 + (mt + 1) * 128],
                                          bap(p6_xf + k8 * 1024, 128, 512),
                                          kc == 0, kc == 15))
                        mmg(items, [r_u0, r_u1, RA(p6_xf + k8 * 1024, 1024)],
                            [RP(psA, 0, 2048), RP(psB, 0, 2048)], tag=f"f{ex}_h{mh}_{cg}_{kc}")
                sop(lambda e: e.activation(out=fap(p6_s0, 128, 2048),
                                           in_=psA[0:128, 0:2048], func=AF.Silu),
                    [RP(psA, 0, 2048)], [r_s0], tag=f"f{ex}_si{mh}_{cg}")
                vop(lambda e, mh=mh, cg=cg: e.tensor_mul(
                    bap(p6_it + (mh * 2 + cg) * 4096, 128, 2048),
                    fap(p6_s0, 128, 2048), psB[0:128, 0:2048]),
                    [r_s0, RP(psB, 0, 2048)], [RA(p6_it + (mh * 2 + cg) * 4096, 4096)],
                    tag=f"f{ex}_im{mh}_{cg}")
        dma("gpsimd", bap(p6_wo, 128, 8192),
            bass.AP(wo, ex * MM * DIM, [[DIM, 128], [128 * DIM, 4], [1, 2048]]),
            NR, [r_wo], tag=f"f{ex}_wo0")
        dma("gpsimd", bap(p6_u, 128, 8192),
            bass.AP(wo, ex * MM * DIM + 512 * DIM,
                    [[DIM, 128], [128 * DIM, 4], [1, 2048]]),
            NR, [r_u0], tag=f"f{ex}_wo1")
        for ct in range(8):
            ps = psA if ct % 2 == 0 else psB
            rps = RP(ps, 0, 2048)
            cg = ct // 4
            for half in range(2):
                items = []
                for mcl in range(4):
                    mc = half * 4 + mcl
                    mhh, mtt = mc // 4, mc % 4
                    lhsT = bap(p6_it + (mhh * 2 + cg) * 4096, 128, 2048)[
                        :, mtt * 512 + (ct % 4) * 128: mtt * 512 + (ct % 4) * 128 + 128]
                    uoff = p6_wo if half == 0 else p6_u
                    for dg in range(4):
                        items.append((ps[0:128, dg * 512:(dg + 1) * 512], lhsT,
                                      bap(uoff, 128, 8192)[:, mcl * 2048 + dg * 512:
                                                           mcl * 2048 + (dg + 1) * 512],
                                      mc == 0, mc == 7))
                mmg(items, [r_it, (r_wo if half == 0 else r_u0)], [rps],
                    tag=f"f{ex}_o{ct}_{half}")
            r_oe = RA(p6_oe, 4096)
            vop(lambda e, ps=ps: e.tensor_copy(bap(p6_oe, 128, 2048),
                                               ps[0:128, 0:2048]),
                [rps], [r_oe], tag=f"f{ex}_oe{ct}")
            dma("sync", outsorted.ap()[ex * CAP + ct * 128:ex * CAP + (ct + 1) * 128, :],
                bap(p6_oe, 128, 2048), [r_oe],
                [RD(outsorted, ex * CAP + ct * 128, ex * CAP + (ct + 1) * 128)],
                tag=f"f{ex}_os{ct}")
    A.release()

    A.mark()
    p7_yb = [A.take(4096 * 4) for _ in range(2)]
    p7_zs = A.take(4096 * 4)
    p7_zbb = [A.take(4096 * 2) for _ in range(2)]
    p7_ecb = A.take(128 * 2)
    p7_ex = [A.take(512 * 4) for _ in range(2)]
    p7_t1 = A.take(512 * 4)
    p7_sq = A.take(4096 * 2)
    p7_st = A.take(16)
    p7_yn = A.take(4096 * 2)
    r_7zs = RA(p7_zs, 16384)
    r_7ecb = RA(p7_ecb, 256, 0, 64)
    r_7t1 = RA(p7_t1, 2048)
    r_7sq = RA(p7_sq, 8192)
    r_7st = RA(p7_st, 16)
    r_7yn = RA(p7_yn, 8192)

    for c in range(NCHUNK):
        p7_y = p7_yb[c % 2]
        p7_zb = p7_zbb[c % 2]
        r_7y = RA(p7_y, 16384)
        r_7zb = RA(p7_zb, 8192)
        dma("sync", fap(p7_y, 128, 4096), ylocal_dram.ap()[c * 128:(c + 1) * 128, :],
            [RD(ylocal_dram, c * 128, (c + 1) * 128)], [r_7y], tag=f"p7y{c}")
        dma("scalar", bap(p7_zb, 128, 4096), z_dram.ap()[c * 128:(c + 1) * 128, :],
            [RD(z_dram, c * 128, (c + 1) * 128)], [r_7zb], tag=f"p7z{c}")
        sop(lambda e, p7_zb=p7_zb: e.activation(out=fap(p7_zs, 128, 4096),
                                                in_=bap(p7_zb, 128, 4096),
                                                func=AF.Silu),
            [r_7zb], [r_7zs], tag=f"p7s{c}")
        vop(lambda e, c=c: e.tensor_scalar(
            out=bap(p7_ecb, 64, 128),
            in0=bap(ey_fmb, 64, 2048)[:, c * 128:(c + 1) * 128],
            scalar1=fap(cdp_fm + c * 4, 64, 1), scalar2=None, op0=ALU.mult),
            [r_eyfmb, r_cdpfm], [r_7ecb], tag=f"p7e{c}")
        for g in range(8):
            c0 = (g % 2) * 512
            e0 = 1024 + (g % 2) * 512
            mmg([(psB[0:128, c0:c0 + 512],
                  bap(cfm, 128, 2048)[:, c * 128:(c + 1) * 128],
                  bap(sinb, 128, 4096)[:, g * 512:(g + 1) * 512], True, True)],
                [r_cfm, r_sinb], [RP(psB, c0, c0 + 512)], tag=f"p7c{c}_{g}")
            mmg([(psB[0:128, e0:e0 + 512], bap(p7_ecb, 64, 128),
                  bap(c_rsel, 64, HID)[:, g * 512:(g + 1) * 512], True, True)],
                [r_7ecb, r_rsel], [RP(psB, e0, e0 + 512)], tag=f"p7x{c}_{g}")
            ex_ = p7_ex[g % 2]
            r_ex = RA(ex_, 2048)
            sop(lambda e, ex_=ex_, e0=e0: e.copy(fap(ex_, 128, 512),
                                                 psB[0:128, e0:e0 + 512]),
                [RP(psB, e0, e0 + 512)], [r_ex], tag=f"p7xc{c}_{g}")
            vop(lambda e, ex_=ex_, c0=c0: e.tensor_mul(
                fap(p7_t1, 128, 512), psB[0:128, c0:c0 + 512], fap(ex_, 128, 512)),
                [RP(psB, c0, c0 + 512), r_ex], [r_7t1], tag=f"p7m{c}_{g}")
            vop(lambda e, g=g, p7_y=p7_y: e.tensor_add(
                fap(p7_y + g * 2048, 128, 512), fap(p7_y + g * 2048, 128, 512),
                fap(p7_t1, 128, 512)), [r_7t1, r_7y], [r_7y], tag=f"p7a{c}_{g}")
        vop(lambda e, p7_y=p7_y: e.tensor_mul(fap(p7_y, 128, 4096),
                                              fap(p7_y, 128, 4096),
                                              fap(p7_zs, 128, 4096)),
            [r_7y, r_7zs], [r_7y], tag=f"p7g{c}")
        sop(lambda e, p7_y=p7_y: e.activation(out=bap(p7_sq, 128, 4096),
                                              in_=fap(p7_y, 128, 4096),
                                              func=AF.Square,
                                              accum_out=fap(p7_st, 128, 1)),
            [r_7y], [r_7sq, r_7st], tag=f"p7q{c}")
        vop(lambda e: e.tensor_scalar(out=fap(p7_st + 4, 128, 1),
                                      in0=fap(p7_st, 128, 1), scalar1=1.0 / 4096,
                                      scalar2=1e-5, op0=ALU.mult, op1=ALU.add),
            [r_7st], [r_7st], tag=f"p7v{c}")
        sop(lambda e: e.activation(out=fap(p7_st + 8, 128, 1),
                                   in_=fap(p7_st + 4, 128, 1), func=AF.Ln),
            [r_7st], [r_7st], tag=f"p7l{c}")
        sop(lambda e: e.activation(out=fap(p7_st + 8, 128, 1),
                                   in_=fap(p7_st + 8, 128, 1), func=AF.Exp,
                                   scale=-0.5), [r_7st], [r_7st], tag=f"p7r{c}")
        vop(lambda e, p7_y=p7_y: e.tensor_scalar(out=bap(p7_yn, 128, 4096),
                                                 in0=fap(p7_y, 128, 4096),
                                                 scalar1=fap(p7_st + 8, 128, 1),
                                                 scalar2=None, op0=ALU.mult),
            [r_7y, r_7st], [r_7yn], tag=f"p7n{c}")
        dma("sync", ynorm_dram.ap()[c * 128:(c + 1) * 128, :], bap(p7_yn, 128, 4096),
            [r_7yn], [RD(ynorm_dram, c * 128, (c + 1) * 128)], tag=f"p7st{c}")
    A.release()

    A.mark()
    p8_yf = [A.take(4096 * 2) for _ in range(2)]
    p8_ws = A.take(8 * 2048 * 2)
    p8_ev = [A.take(2048 * 4) for _ in range(2)]
    r_ws = RA(p8_ws, 32768)

    for hcg in range(4):
        for h8 in range(8):
            hc = hcg * 8 + h8
            dma("gpsimd", bap(p8_ws + h8 * 4096, 128, 2048),
                bass.AP(woutT, hc * 128 * DIM, [[DIM, 128], [1, 2048]]),
                NR, [RA(p8_ws + h8 * 4096, 4096)], tag=f"s{hcg}_w{h8}")
            vop(lambda e, h8=h8, hc=hc: e.tensor_scalar(
                out=bap(p8_ws + h8 * 4096, 128, 2048),
                in0=bap(p8_ws + h8 * 4096, 128, 2048),
                scalar1=fap(c_nw + hc * 4, 128, 1), scalar2=None, op0=ALU.mult),
                [RA(p8_ws + h8 * 4096, 4096), r_nw], [RA(p8_ws + h8 * 4096, 4096)],
                tag=f"s{hcg}_n{h8}")
        for tg in range(4):
            yf = p8_yf[tg % 2]
            r_yf = RA(yf, 8192)
            for h8 in range(8):
                hc = hcg * 8 + h8
                dmaT("scalar", bap(yf + h8 * 1024, 128, 512),
                     ynorm_dram.ap()[tg * 512:(tg + 1) * 512, hc * 128:(hc + 1) * 128],
                     [RD(ynorm_dram, tg * 512, (tg + 1) * 512)],
                     [RA(yf + h8 * 1024, 1024)], tag=f"s{hcg}_y{tg}_{h8}")
            for tile in range(4):
                tt = tg * 4 + tile
                ps = psA if tile % 2 == 0 else psB
                rps = RP(ps, 0, 2048)
                items = []
                for h8 in range(8):
                    for dg in range(4):
                        items.append((ps[0:128, dg * 512:(dg + 1) * 512],
                                      bap(yf + h8 * 1024, 128, 512)[
                                          :, tile * 128:(tile + 1) * 128],
                                      bap(p8_ws + h8 * 4096, 128, 2048)[
                                          :, dg * 512:(dg + 1) * 512],
                                      h8 == 0, h8 == 7))
                mmg(items, [r_yf, r_ws], [rps], tag=f"s{hcg}_m{tt}")
                ev = p8_ev[tile % 2]
                r_ev = RA(ev, 8192)
                vop(lambda e, ev=ev, ps=ps: e.tensor_copy(fap(ev, 128, 2048),
                                                          ps[0:128, 0:2048]),
                    [rps], [r_ev], tag=f"s{hcg}_e{tt}")
                if hcg == 0:
                    dma("gpsimd", s_dram.ap()[tt * 128:(tt + 1) * 128, :],
                        fap(ev, 128, 2048), [r_ev],
                        [RD(s_dram, tt * 128, (tt + 1) * 128)], tag=f"s{hcg}_s{tt}")
                else:
                    dma("gpsimd", s_dram.ap()[tt * 128:(tt + 1) * 128, :],
                        fap(ev, 128, 2048), [r_ev],
                        [RD(s_dram, tt * 128, (tt + 1) * 128)],
                        accum_op=ALU.add, tag=f"s{hcg}_s{tt}")
    A.release()

    A.mark()
    p9_s = [A.take(2048 * 4) for _ in range(2)]
    p9_g1 = [A.take(2048 * 2) for _ in range(2)]
    p9_g2 = [A.take(2048 * 2) for _ in range(2)]
    p9_x = [A.take(2048 * 4) for _ in range(2)]
    p9_gg = [A.take(2048 * 4) for _ in range(2)]
    bgg2_off = A.take(2048 * 4)
    r_gg2 = RA(bgg2_off, 8192)
    for tt in range(16):
        bs, bg1, bg2, bx, bgg = (p9_s[tt % 2], p9_g1[tt % 2], p9_g2[tt % 2],
                                 p9_x[tt % 2], p9_gg[tt % 2])
        r_s = RA(bs, 8192)
        r_g1, r_g2 = RA(bg1, 4096), RA(bg2, 4096)
        r_x, r_gg = RA(bx, 8192), RA(bgg, 8192)
        dma("sync", fap(bs, 128, 2048), s_dram.ap()[tt * 128:(tt + 1) * 128, :],
            [RD(s_dram, tt * 128, (tt + 1) * 128)], [r_s], tag=f"o{tt}_s")
        kb.op("gpsimd", lambda e, bg1=bg1, tt=tt: e.indirect_dma_start(
            out=bap(bg1, 128, 2048), out_offset=None, in_=outsorted.ap(),
            in_offset=IOA(ix_h[:, 16 + tt * 2:17 + tt * 2], 0)),
            reads=[RX(16 + tt * 2, 17 + tt * 2), RD(outsorted, 0, 8192)],
            writes=[r_g1], dma=True, tag=f"o{tt}_g1")
        kb.op("gpsimd", lambda e, bg2=bg2, tt=tt: e.indirect_dma_start(
            out=bap(bg2, 128, 2048), out_offset=None, in_=outsorted.ap(),
            in_offset=IOA(ix_h[:, 17 + tt * 2:18 + tt * 2], 0)),
            reads=[RX(17 + tt * 2, 18 + tt * 2), RD(outsorted, 0, 8192)],
            writes=[r_g2], dma=True, tag=f"o{tt}_g2")
        dma("sync", fap(bx, 128, 2048), x_in.ap()[3 + tt * 128:3 + (tt + 1) * 128, :],
            NR, [r_x], tag=f"o{tt}_x")
        vop(lambda e, bgg=bgg, bg1=bg1, tt=tt: e.tensor_scalar(
            out=fap(bgg, 128, 2048), in0=bap(bg1, 128, 2048),
            scalar1=wap(tt * 2, 1), scalar2=None, op0=ALU.mult),
            [r_g1, r_w12], [r_gg], tag=f"o{tt}_m1")
        vop(lambda e, bg2=bg2, bs=bs, tt=tt: e.tensor_scalar(
            out=fap(bg2 if False else bgg2_off, 128, 2048), in0=bap(bg2, 128, 2048),
            scalar1=wap(tt * 2 + 1, 1), scalar2=None, op0=ALU.mult),
            [r_g2, r_w12], [r_gg2], tag=f"o{tt}_m2a")
        vop(lambda e, bgg=bgg: e.tensor_add(fap(bgg, 128, 2048),
                                            fap(bgg, 128, 2048),
                                            fap(bgg2_off, 128, 2048)),
            [r_gg, r_gg2], [r_gg], tag=f"o{tt}_m2b")
        if DEBUG:
            dma("gpsimd", dbg["gsum"].ap()[tt * 128:(tt + 1) * 128, :],
                fap(bgg, 128, 2048), [r_gg],
                [RD(dbg["gsum"], tt * 128, (tt + 1) * 128)])
            dma("gpsimd", dbg["sproj"].ap()[tt * 128:(tt + 1) * 128, :],
                fap(bs, 128, 2048), [r_s],
                [RD(dbg["sproj"], tt * 128, (tt + 1) * 128)])
        sop(lambda e, bgg=bgg: e.activation(
            out=fap(bgg, 128, 2048), in_=fap(bgg, 128, 2048), func=AF.Silu),
            [r_gg], [r_gg], tag=f"o{tt}_si")
        vop(lambda e, bs=bs, bgg=bgg: e.tensor_mul(
            fap(bs, 128, 2048), fap(bs, 128, 2048), fap(bgg, 128, 2048)),
            [r_s, r_gg], [r_s], tag=f"o{tt}_sg")
        vop(lambda e, bs=bs, bx=bx: e.tensor_add(
            fap(bs, 128, 2048), fap(bs, 128, 2048), fap(bx, 128, 2048)),
            [r_s, r_x], [r_s], tag=f"o{tt}_fx")
        dma("sync", out_t.ap()[tt * 128:(tt + 1) * 128, :], fap(bs, 128, 2048),
            [r_s], [RD(out_t, tt * 128, (tt + 1) * 128)], tag=f"o{tt}_out")
    A.release()

    if DEBUG:
        A.mark()
        db = A.take(4096 * 4)
        r_db = RA(db, 16384)
        def dump(dst, src_t, rows, cols, dt_b):
            for i in range(rows // 128):
                n = cols
                if dt_b:
                    dma("sync", bap(db, 128, n), src_t.ap()[i * 128:(i + 1) * 128, :],
                        [RD(src_t, i * 128, (i + 1) * 128)], [r_db])
                    dma("sync", dst.ap()[i * 128:(i + 1) * 128, :], bap(db, 128, n),
                        [r_db], [RD(dst, i * 128, (i + 1) * 128)])
                else:
                    dma("sync", fap(db, 128, n), src_t.ap()[i * 128:(i + 1) * 128, :],
                        [RD(src_t, i * 128, (i + 1) * 128)], [r_db])
                    dma("sync", dst.ap()[i * 128:(i + 1) * 128, :], fap(db, 128, n),
                        [r_db], [RD(dst, i * 128, (i + 1) * 128)])
        dma("gpsimd", dbg["idx"].ap(), idx_dram.ap(), [RD(idx_dram, 0, 8192)],
            [RD(dbg["idx"], 0, 8192)])
        dbg["xn3"] = DO("dbg_xn3", [2048, DIM], BF16)
        dump(dbg["xn3"], xn_dram, 2048, DIM, True)
        dump(dbg["xbc"], conv_dump, CONV_DIM, T, True)
        dump(dbg["z"], z_dram, T, HID, True)
        dump(dbg["ylocal"], ylocal_dram, T, HID, False)
        dump(dbg["ynorm"], ynorm_dram, T, HID, True)
        dump(dbg["xs"], xsorted, NEXP * CAP, DIM, True)
        dump(dbg["os"], outsorted, NEXP * CAP, DIM, True)
        A.release()
    kb.barrier("sync", {k: v for k, v in kb.counts.items() if k.startswith("d_")})
    kb.barrier("gpsimd", {k: v for k, v in kb.counts.items() if k.startswith("d_")})


    kb.finalize(es)
    return nc, es, dbg


def kernel(L0_attn_x, L0_attn_in_proj_w, L0_attn_conv_w, L0_attn_dt_bias,
           L0_attn_A_log, L0_attn_D, L0_attn_norm_w, L0_attn_out_proj_w,
           L0_ffn_w_gate, L0_ffn_wi_0, L0_ffn_wi_1, L0_ffn_wo):
    x = np.asarray(L0_attn_x)
    winT = np.ascontiguousarray(np.asarray(L0_attn_in_proj_w).T)
    woutT = np.ascontiguousarray(np.asarray(L0_attn_out_proj_w).T)
    ident = np.eye(128, dtype=np.float32)
    tri = np.triu(np.ones((128, 128), dtype=np.float32))
    rsel = np.zeros((64, HID), dtype=np.float32)
    for h in range(64):
        rsel[h, h * 64:(h + 1) * 64] = 1.0
    rsel = rsel.astype(ml_dtypes.bfloat16)

    nc, es, dbg = build_core_program()
    in_maps = []
    for core in range(8):
        b, half = core // 2, core % 2
        halo = (np.zeros((3, DIM), np.float32) if half == 0
                else np.ascontiguousarray(x[b, 2045:2048]))
        own = np.ascontiguousarray(x[b, half * T:(half + 1) * T])
        in_maps.append(dict(
            x2051=np.concatenate([halo, own], axis=0),
            winT=winT, woutT=woutT,
            convw=np.ascontiguousarray(L0_attn_conv_w),
            dt_bias=np.asarray(L0_attn_dt_bias), a_log=np.asarray(L0_attn_A_log),
            d_vec=np.asarray(L0_attn_D),
            norm_w=np.ascontiguousarray(np.asarray(L0_attn_norm_w).reshape(32, 128).T),
            w_gate=np.asarray(L0_ffn_w_gate),
            wi0=np.asarray(L0_ffn_wi_0), wi1=np.asarray(L0_ffn_wi_1),
            wo=np.asarray(L0_ffn_wo),
            ident=ident, tri=tri, rsel=rsel,
            gates=np.array([1.0 - half, float(half)], np.float32),
        ))
    res = run_bass_kernel_spmd(nc, in_maps, core_ids=list(range(8)))
    es.close()
    outs = [np.asarray(r["out"]) for r in res.results]
    full = np.empty((4, 4096, DIM), np.float32)
    for core in range(8):
        b, half = core // 2, core % 2
        full[b, half * T:(half + 1) * T] = outs[core]
    if DEBUG:
        kernel.last_results = res
    return full
